# version 25
# baseline (speedup 1.0000x reference)
import math
import numpy as np
from contextlib import ExitStack

import concourse.bass as bass
import concourse.mybir as mybir
from concourse.bass_utils import run_bass_kernel_spmd

F32 = mybir.dt.float32
BF16 = mybir.dt.bfloat16
AF = mybir.ActivationFunctionType
ALU = mybir.AluOpType
AX = mybir.AxisListType

S = 4096
HID = 2048
OWN = 1024
NDH = 8
NG = 4
SCALE = 128.0 ** -0.5
LI = 0.8 - 0.6 * math.exp(-0.3)
EPS = 1e-5
MASKVAL = -30000.0


def cdiv(a, b):
    return (a + b - 1) // b


class Em:
    def __init__(self, nc, stack):
        self.nc = nc
        self.stack = stack
        self.sems = {}
        self.cnt = {}
        self.step = {}
        for name, step in [("pe", 1), ("ve", 1), ("ac", 1), ("po", 1)]:
            self.sems[name] = stack.enter_context(nc.semaphore(name))
            self.cnt[name] = 0
            self.step[name] = step
        self.seen = {}

    def ensure(self, ch):
        if ch not in self.sems:
            self.sems[ch] = self.stack.enter_context(self.nc.semaphore(ch))
            self.cnt[ch] = 0
            self.step[ch] = 16

    def dma(self, instr, ch):
        if ch not in self.sems:
            self.sems[ch] = self.stack.enter_context(self.nc.semaphore(ch))
            self.cnt[ch] = 0
            self.step[ch] = 16
        return self.inc(instr, ch)

    def inc(self, instr, q, k=1):
        v = self.cnt[q] + self.step[q] * k
        self.cnt[q] = v
        instr.then_inc(self.sems[q], self.step[q] * k)
        return (q, v)

    def tok(self, q):
        return (q, self.cnt[q])

    def wait(self, eng, ename, tok):
        if tok is None:
            return
        q, v = tok
        if v <= 0:
            return
        key = (ename, q)
        if self.seen.get(key, 0) >= v:
            return
        eng.wait_ge(self.sems[q], v)
        self.seen[key] = v

    def snap_state(self):
        return (dict(self.cnt), dict(self.seen))

    def restore_state(self, st):
        self.cnt = dict(st[0])
        self.seen = dict(st[1])


def build_program():
    nc = bass.Bass("TRN2", target_bir_lowering=False, debug=False)

    x_full = nc.dram_tensor("x_full", [S, HID], F32, kind="ExternalInput").ap()
    x_own = nc.dram_tensor("x_own", [OWN, HID], F32, kind="ExternalInput").ap()
    wqkvT = nc.dram_tensor("wqkvT", [HID, 4096], F32, kind="ExternalInput").ap()
    bqkv = nc.dram_tensor("bqkv", [1, 4096], F32, kind="ExternalInput").ap()
    lrow_in = nc.dram_tensor("lrows", [4, 128], F32, kind="ExternalInput").ap()
    subln_in = nc.dram_tensor("subln", [1, 256], F32, kind="ExternalInput").ap()
    outwT_in = nc.dram_tensor("outwT", [HID, HID], F32, kind="ExternalInput").ap()
    outb_in = nc.dram_tensor("outb", [1, HID], F32, kind="ExternalInput").ap()
    at_in = nc.dram_tensor("a_t", [HID, 1024], F32, kind="ExternalInput").ap()
    b2_in = nc.dram_tensor("b2", [1024, HID], F32, kind="ExternalInput").ap()
    wt_in = nc.dram_tensor("w_t", [HID, HID], F32, kind="ExternalInput").ap()
    fb_in = nc.dram_tensor("fbias", [1, HID], F32, kind="ExternalInput").ap()
    y_out = nc.dram_tensor("y", [OWN, HID], F32, kind="ExternalOutput").ap()

    k_b = nc.dram_tensor("k_b", [1024, S], BF16).ap()
    v_b = nc.dram_tensor("v_b", [S, 1024], BF16).ap()
    q_b = nc.dram_tensor("q_b", [2048, OWN], BF16).ap()
    o_b = nc.dram_tensor("o_b", [OWN, 2048], BF16).ap()
    s_b = nc.dram_tensor("s_b", [OWN, HID], BF16).ap()
    wfm_b = nc.dram_tensor("wfm_b", [1024, OWN], BF16).ap()

    stack = ExitStack()
    sb = lambda n, s_, d: stack.enter_context(nc.sbuf_tensor(n, s_, d))
    em = Em(nc, stack)
    EV = [nc.tensor, nc.vector, nc.scalar, nc.sync, nc.gpsimd]

    ident_bf = sb("ident_bf", [128, 128], BF16)
    cmask = sb("cmask", [128, 128], BF16)
    ones_row = sb("ones_row", [1, 128], F32)
    ones_col_bf = sb("ones_col_bf", [128, 1], BF16)
    bq_cols = sb("bq_cols", [128, 32], F32)
    lamneg = sb("lamneg", [128, 1], F32)
    subln_sc = sb("subln_sc", [128, 256], F32)
    qn2row = sb("qn2row", [1, 128], F32)
    knmax = sb("knmax", [1, 256], F32)
    bound_bc = sb("bound_bc", [128, 128], F32)
    eps_col = sb("eps_col", [128, 1], F32)

    def emit_ln(x_tile, scr, stat, out_bf):
        em.wait(nc.scalar, "s", scr)
        nc.scalar.activation(out_bf[:], x_tile, AF.Copy, accum_out=stat[:, 0:1])
        nc.scalar.drain()
        ev_a = em.inc(nc.scalar.activation(out_bf[:], x_tile, AF.Square,
                                           accum_out=stat[:, 1:2]), "ac")
        em.wait(nc.vector, "v", ev_a)
        nc.vector.tensor_scalar_mul(stat[:, 2:3], stat[:, 0:1], 1.0 / HID)
        nc.vector.tensor_scalar_mul(stat[:, 3:4], stat[:, 1:2], 1.0 / HID)
        nc.vector.drain()
        nc.vector.tensor_mul(stat[:, 4:5], stat[:, 2:3], stat[:, 2:3])
        nc.vector.drain()
        ev_v = em.inc(nc.vector.tensor_sub(stat[:, 5:6], stat[:, 3:4], stat[:, 4:5]), "ve")
        em.wait(nc.scalar, "s", ev_v)
        nc.scalar.activation(stat[:, 6:7], stat[:, 5:6], AF.Ln, bias=eps_col[:])
        nc.scalar.drain()
        ev_a = em.inc(nc.scalar.activation(stat[:, 7:8], stat[:, 6:7], AF.Exp,
                                           scale=-0.5), "ac")
        em.wait(nc.vector, "v", ev_a)
        nc.vector.tensor_mul(stat[:, 4:5], stat[:, 2:3], stat[:, 7:8])
        nc.vector.drain()
        nc.vector.tensor_scalar_mul(stat[:, 4:5], stat[:, 4:5], -1.0)
        nc.vector.drain()
        return em.inc(nc.vector.tensor_scalar(out_bf[:], x_tile, stat[:, 7:8],
                                              stat[:, 4:5], op0=ALU.mult,
                                              op1=ALU.add), "ve")

    class TrPool:

        def __init__(self, ps2):
            self.ps = ps2
            self.free = [None, None]
            self.i = 0

        def run(self, srcs, dsts, extra_v_wait=None):
            ev = None
            gi = 0
            for g0 in range(0, len(srcs), 4):
                grp = srcs[g0:g0 + 4]
                bank = self.i % 2
                self.i += 1
                em.wait(nc.tensor, "t", self.free[bank])
                for j, s_ in enumerate(grp):
                    tt = nc.tensor.transpose(self.ps[bank][:][:, j, :], s_, ident_bf[:])
                ev_pe = em.inc(tt, "pe")
                em.wait(nc.vector, "v", ev_pe)
                if extra_v_wait is not None:
                    em.wait(nc.vector, "v", extra_v_wait)
                ev = em.inc(nc.vector.tensor_copy(dsts[gi],
                                                  self.ps[bank][:][:, 0:len(grp), :]), "ve")
                self.free[bank] = ev
                gi += 1
            return ev

    def emit_row_bcast(row_ap, psum, dst, ncols, scale=None):
        ev = None
        for j0 in range(0, ncols, 512):
            w = min(512, ncols - j0)
            em.wait(nc.tensor, "t", em.tok("ve"))
            t = nc.tensor.matmul(psum[:, 0:w], ones_row[:], row_ap[0:1, j0:j0 + w],
                                 start=True, stop=True)
            ev_pe = em.inc(t, "pe")
            em.wait(nc.vector, "v", ev_pe)
            if scale is None:
                ins = nc.vector.tensor_copy(dst[:, j0:j0 + w], psum[:, 0:w])
            else:
                ins = nc.vector.tensor_scalar_mul(dst[:, j0:j0 + w], psum[:, 0:w], scale)
            ev = em.inc(ins, "ve")
        return ev

    def phase_barrier(tokens):
        for eng, en in [(nc.sync, "y"), (nc.gpsimd, "g"), (nc.vector, "v"),
                        (nc.scalar, "s"), (nc.tensor, "t")]:
            for tk in tokens:
                em.wait(eng, en, tk)

    g = nc.gpsimd
    g.memset(ident_bf[:], 0.0)
    g.memset(cmask[:], 0.0)
    g.memset(ones_row[:], 1.0)
    g.memset(ones_col_bf[:], 1.0)
    g.memset(eps_col[:], float(EPS))
    g.drain()
    g.affine_select(out=ident_bf[:], in_=ident_bf[:], compare_op=ALU.not_equal,
                    fill=1.0, base=0, pattern=[[-1, 128]], channel_multiplier=1)
    tm = g.affine_select(out=cmask[:], in_=cmask[:], compare_op=ALU.is_ge,
                         fill=MASKVAL, base=0, pattern=[[-1, 128]],
                         channel_multiplier=1)
    ev_const = em.inc(tm, "po")

    pid = nc.partition_id(engines=[e.engine for e in EV])
    idxv = nc.snap(pid % 4, engines=[e.engine for e in EV])

    with ExitStack() as p2kv:
        lcols = p2kv.enter_context(nc.sbuf_tensor("lcols", [128, 4], F32))
        dots = p2kv.enter_context(nc.sbuf_tensor("dots", [1, 4], F32))
        lam_t = p2kv.enter_context(nc.sbuf_tensor("lam_t", [1, 4], F32))
        xk0 = p2kv.enter_context(nc.sbuf_tensor("xk0", [128, HID], F32))
        xk1 = p2kv.enter_context(nc.sbuf_tensor("xk1", [128, HID], F32))
        stat2 = p2kv.enter_context(nc.sbuf_tensor("stat2", [128, 16], F32))
        xn_tm2 = p2kv.enter_context(nc.sbuf_tensor("xn_tm2", [128, HID], BF16))
        xn_tm2_b = p2kv.enter_context(nc.sbuf_tensor("xn_tm2_b", [128, HID], BF16))
        panelA = p2kv.enter_context(nc.sbuf_tensor("panelA", [128, 16, 2048], BF16))
        panelB = p2kv.enter_context(nc.sbuf_tensor("panelB", [128, 16, 2048], BF16))
        xnq_fm = p2kv.enter_context(nc.sbuf_tensor("xnq_fm", [128, 16, 512], BF16))
        wsl20 = p2kv.enter_context(nc.sbuf_tensor("wsl20", [128, 16, 128], BF16))
        wsl21 = p2kv.enter_context(nc.sbuf_tensor("wsl21", [128, 16, 128], BF16))
        wslq0 = p2kv.enter_context(nc.sbuf_tensor("wslq0", [128, 16, 128], BF16))
        wslq1 = p2kv.enter_context(nc.sbuf_tensor("wslq1", [128, 16, 128], BF16))
        kvstage0 = p2kv.enter_context(nc.sbuf_tensor("kvstage0", [128, 1024], BF16))
        kvstage1 = p2kv.enter_context(nc.sbuf_tensor("kvstage1", [128, 1024], BF16))
        ksqA = p2kv.enter_context(nc.sbuf_tensor("ksqA", [128, 1024], BF16))
        ksqB = p2kv.enter_context(nc.sbuf_tensor("ksqB", [128, 1024], BF16))
        vt0 = p2kv.enter_context(nc.sbuf_tensor("vt0", [128, 8, 128], BF16))
        vt1 = p2kv.enter_context(nc.sbuf_tensor("vt1", [128, 8, 128], BF16))
        qstage = p2kv.enter_context(nc.sbuf_tensor("qstage", [128, 512], BF16))
        qsqA = p2kv.enter_context(nc.sbuf_tensor("qsqA", [128, 512], BF16))
        qsqB = p2kv.enter_context(nc.sbuf_tensor("qsqB", [128, 512], BF16))
        tr_ps20 = p2kv.enter_context(nc.psum_tensor("tr_ps20", [128, 4, 128], BF16))
        tr_ps21 = p2kv.enter_context(nc.psum_tensor("tr_ps21", [128, 4, 128], BF16))
        kv_ps0 = p2kv.enter_context(nc.psum_tensor("kv_ps0", [128, 1024], F32))
        kv_ps1 = p2kv.enter_context(nc.psum_tensor("kv_ps1", [128, 1024], F32))
        n_ps20 = p2kv.enter_context(nc.psum_tensor("n_ps20", [1, 512], F32))
        p0_sub = ExitStack()
        p0_ps = p0_sub.enter_context(nc.psum_tensor("p0_ps", [128, 256], F32))
        xk = [xk0, xk1]; wsl2 = [wsl20, wsl21]; wslq = [wslq0, wslq1]
        kvstage = [kvstage0, kvstage1]; ksq = [ksqA, ksqB]; qsq = [qsqA, qsqB]
        vt = [vt0, vt1]
        tr_ps2 = [tr_ps20, tr_ps21]; kv_ps = [kv_ps0, kv_ps1]
        panel = [panelA, panelB]
        xn_tm2b = [xn_tm2, xn_tm2_b]

        em.wait(nc.tensor, "t", ev_const)
        em.wait(nc.scalar, "s", ev_const)
        em.wait(nc.vector, "v", ev_const)
        trp = TrPool(tr_ps2)

        ln_last = [None, None]
        tr_pe2 = [None, None]
        panel_done = [None, None]
        xnq_done = [None, None]
        job_tok = {}
        job_par = {}
        jseq = [0]

        def issue_job(job):
            par = jseq[0] % 2
            jseq[0] += 1
            job_par[job] = par
            em.wait(nc.sync, "y", ln_last[par])
            kind, blk_, t = job
            if kind == "p":
                src, r0 = x_full, (blk_ * 16 + t) * 128
            else:
                src, r0 = x_own, blk_ * 512 + t * 128
            tka = em.dma(nc.sync.dma_start(
                out=xk[par][:, 0:1024], in_=src[r0:r0 + 128, 0:1024]), f"xla{par}")
            tkb = em.dma(nc.sync.dma_start(
                out=xk[par][:, 1024:2048], in_=src[r0:r0 + 128, 1024:2048]), f"xlb{par}")
            job_tok[job] = (tka, tkb)

        def ln_job(job):
            par = job_par[job]
            for tk in job_tok[job]:
                em.wait(nc.scalar, "s", tk)
                em.wait(nc.vector, "v", tk)
            ev_v = emit_ln(xk[par][:], tr_pe2[par], stat2[:, par * 8:par * 8 + 8],
                           xn_tm2b[par])
            ln_last[par] = ev_v
            return ev_v

        def tr_job(job, ev_v):
            par = job_par[job]
            kind, blk_, t = job
            em.wait(nc.tensor, "t", ev_v)
            if kind == "p":
                dsts = [panel[blk_ % 2][:, g0:g0 + 4, t * 128:(t + 1) * 128]
                        for g0 in range(0, 16, 4)]
            else:
                dsts = [xnq_fm[:, g0:g0 + 4, t * 128:(t + 1) * 128]
                        for g0 in range(0, 16, 4)]
            ev = trp.run([xn_tm2b[par][:, hc * 128:(hc + 1) * 128] for hc in range(16)],
                         dsts)
            tr_pe2[par] = em.tok("pe")
            if kind == "p":
                panel_done[blk_ % 2] = ev
            else:
                xnq_done[blk_] = ev
            return ev

        def run_jobs(jobs):
            for j in jobs[:2]:
                issue_job(j)
            prev = None
            for i, j in enumerate(jobs):
                ev_v = ln_job(j)
                if i + 2 < len(jobs):
                    issue_job(jobs[i + 2])
                if prev is not None:
                    tr_job(*prev)
                prev = (j, ev_v)
            tr_job(*prev)

        em.dma(nc.sync.dma_start(
            out=bq_cols[:], in_=bqkv.rearrange("o (j p) -> p (o j)", p=128),
            allow_slow_non_contiguous=True), "p0")
        em.dma(nc.sync.dma_start(
            out=lcols[:], in_=lrow_in.rearrange("r c -> c r"),
            allow_slow_non_contiguous=True), "p0")
        em.dma(nc.sync.dma_start(out=subln_sc[0:1, :], in_=subln_in[:]), "p0")
        tok_p0 = em.tok("p0")
        em.wait(nc.tensor, "t", tok_p0)
        nc.tensor.matmul(p0_ps[0:1, 250:251], lcols[:, 0:1], lcols[:, 1:2],
                         start=True, stop=True)
        t_ = nc.tensor.matmul(p0_ps[0:1, 251:252], lcols[:, 2:3], lcols[:, 3:4],
                              start=True, stop=True)
        ev_pe = em.inc(t_, "pe")
        em.wait(nc.scalar, "s", ev_pe)
        em.inc(nc.scalar.activation(dots[:, 0:2], p0_ps[0:1, 250:252], AF.Exp), "ac")
        ev_p0_ac = em.tok("ac")
        em.wait(nc.vector, "v", ev_p0_ac)
        nc.vector.tensor_sub(lam_t[:, 0:1], dots[:, 0:1], dots[:, 1:2])
        nc.vector.drain()
        ev_lam = em.inc(nc.vector.tensor_scalar(lam_t[:, 1:2], lam_t[:, 0:1],
                                                float(LI), -1.0,
                                                op0=ALU.add, op1=ALU.mult), "ve")
        em.wait(nc.tensor, "t", ev_lam)
        t_ = nc.tensor.matmul(p0_ps[:, 36:37], ones_row[:], lam_t[:, 1:2],
                              start=True, stop=True)
        ev_pe = em.inc(t_, "pe")
        em.wait(nc.vector, "v", ev_pe)
        ev_lnb = em.inc(nc.vector.tensor_copy(lamneg[:], p0_ps[:, 36:37]), "ve")
        em.wait(nc.tensor, "t", ev_lnb)
        t_ = nc.tensor.matmul(p0_ps[:, 0:256], ones_row[:], subln_sc[0:1, :],
                              start=True, stop=True)
        ev_pe = em.inc(t_, "pe")
        em.wait(nc.vector, "v", ev_pe)
        em.inc(nc.vector.tensor_scalar_mul(subln_sc[:], p0_ps[:, 0:256],
                                           float(1.0 - LI)), "ve")
        ev_p0 = em.tok("ve")
        p0_sub.close()
        q_ps = p2kv.enter_context(nc.psum_tensor("q_ps", [128, 512], F32))

        w_tok = [None, None]
        wq_tok = [None, None]
        kv_ps_free = [None, None]
        q_ps_free = None
        k_dma = [None, None]
        k_sq_done = [None, None]
        ksq_mm = [None, None]
        q_dma = None
        q_sq_done = None
        qsq_mm = [None, None]
        nrm_free = None
        vt_dma = [None, None]
        v_tr_done = [None, None]
        norm_pending = []

        def flush_norms(keep):
            nonlocal nrm_free
            while len(norm_pending) > keep:
                ent = norm_pending.pop(0)
                if ent[0] == 'k':
                    _, dsl0, half0, blk0, ev_sq0 = ent
                    for j in range(2):
                        em.wait(nc.tensor, "t", ev_sq0)
                        em.wait(nc.tensor, "t", nrm_free)
                        t0 = nc.tensor.matmul(n_ps20[0:1, :], ones_col_bf[:],
                                              ksq[half0][:, j * 512:(j + 1) * 512],
                                              start=True, stop=True)
                        ev_pe0 = em.inc(t0, "pe")
                        em.wait(nc.vector, "v", ev_pe0)
                        blkpos = blk0 * 16 + half0 * 8 + j * 4
                        nrm_free = em.inc(nc.vector.tensor_reduce(
                            knmax[0:1, dsl0 * 32 + blkpos:dsl0 * 32 + blkpos + 4],
                            n_ps20[0:1, :].rearrange("o (a b) -> o a b", b=128),
                            axis=AX.X, op=ALU.max), "ve")
                    ksq_mm[half0] = em.tok("pe")
                else:
                    _, dsl0, blk0, par0, ev_sq0 = ent
                    em.wait(nc.tensor, "t", ev_sq0)
                    em.wait(nc.tensor, "t", nrm_free)
                    t0 = nc.tensor.matmul(n_ps20[0:1, :], ones_col_bf[:],
                                          qsq[par0][:, :], start=True, stop=True)
                    ev_pe0 = em.inc(t0, "pe")
                    em.wait(nc.vector, "v", ev_pe0)
                    nrm_free = em.inc(nc.vector.tensor_reduce(
                        qn2row[0:1, dsl0 * 8 + blk0 * 4:dsl0 * 8 + blk0 * 4 + 4],
                        n_ps20[0:1, :].rearrange("o (a b) -> o a b", b=128),
                        axis=AX.X, op=ALU.max), "ve")
                    qsq_mm[par0] = em.tok("pe")

        def preload_weights(first):
            if not first:
                em.wait(nc.gpsimd, "g", em.tok("pe"))
            for d2 in range(2):
                w_tok[d2] = em.dma(nc.gpsimd.dma_start(
                    out=wsl2[d2][:],
                    in_=wqkvT[:, 2048 + d2 * 128:2048 + (d2 + 1) * 128]
                        .rearrange("(hc p) d -> p hc d", p=128)), f"wl{d2}")
                wq_tok[d2] = em.dma(nc.gpsimd.dma_start(
                    out=wslq[d2][:],
                    in_=wqkvT[:, d2 * 128:(d2 + 1) * 128]
                        .rearrange("(hc p) d -> p hc d", p=128)), f"ql{d2}")

        def emit_kv_evac(blk, dsl, half, ev_pe_mm):
            em.wait(nc.vector, "v", ev_pe_mm)
            em.wait(nc.vector, "v", em.tok("p0"))
            em.wait(nc.vector, "v", k_dma[half])
            em.wait(nc.vector, "v", k_sq_done[half])
            em.wait(nc.vector, "v", v_tr_done[half])
            ev_st = em.inc(nc.vector.tensor_scalar(
                kvstage[half][:], kv_ps[half][:], bq_cols[:, 16 + dsl:17 + dsl],
                None, op0=ALU.add), "ve")
            kv_ps_free[half] = ev_st
            if dsl < 8:
                em.wait(nc.sync, "y", ev_st)
                k_dma[half] = em.dma(nc.sync.dma_start(
                    out=k_b[dsl * 128:(dsl + 1) * 128,
                            blk * 2048 + half * 1024:blk * 2048 + half * 1024 + 1024],
                    in_=kvstage[half][:]), f"ksb{half}")
                while any(e[0] == 'k' and e[2] == half for e in norm_pending):
                    flush_norms(len(norm_pending) - 1)
                em.wait(nc.scalar, "s", ev_st)
                em.wait(nc.scalar, "s", ksq_mm[half])
                ev_sq = em.inc(nc.scalar.activation(ksq[half][:], kvstage[half][:],
                                                    AF.Square), "ac")
                k_sq_done[half] = ev_sq
                norm_pending.append(('k', dsl, half, blk, ev_sq))
            else:
                em.wait(nc.tensor, "t", ev_st)
                ev_tr = trp.run([kvstage[half][:, tt * 128:(tt + 1) * 128]
                                 for tt in range(8)],
                                [vt[half][:, g0:g0 + 4, :] for g0 in range(0, 8, 4)],
                                extra_v_wait=vt_dma[half])
                v_tr_done[half] = em.tok("pe")
                em.wait(nc.sync, "y", ev_tr)
                vt_dma[half] = em.dma(nc.sync.dma_start(
                    out=v_b.rearrange("(a p) e -> p a e", p=128)
                        [:, blk * 16 + half * 8:blk * 16 + half * 8 + 8,
                         (dsl - 8) * 128:(dsl - 8) * 128 + 128],
                    in_=vt[half][:]), f"vsb{half}")

        preload_weights(True)

        J0 = ([("p", 0, t) for t in range(12)]
              + [("o", 0, 0), ("p", 0, 12), ("o", 0, 1), ("p", 0, 13),
                 ("o", 0, 2), ("p", 0, 14), ("o", 0, 3), ("p", 0, 15)])
        run_jobs(J0)

        J1p = [("p", 1, t) for t in range(16)]
        issue_job(J1p[0])
        issue_job(J1p[1])

        for blk in range(2):
            pn = panel[blk % 2]
            for dsl in range(16):
                par = dsl % 2
                ev_panel = panel_done[blk % 2]
                if blk == 0:
                    pl_ev = ln_job(J1p[dsl])
                    if dsl + 2 < 16:
                        issue_job(J1p[dsl + 2])
                em.wait(nc.tensor, "t", w_tok[par])
                em.wait(nc.tensor, "t", ev_panel)
                em.wait(nc.tensor, "t", kv_ps_free[0])
                for h in range(16):
                    for j in range(2):
                        t_ = nc.tensor.matmul(kv_ps[0][:][:, j * 512:(j + 1) * 512],
                                              wsl2[par][:][:, h, :],
                                              pn[:, h, j * 512:j * 512 + 512],
                                              start=(h == 0), stop=(h == 15))
                ev_pe_h0 = em.inc(t_, "pe")
                em.wait(nc.tensor, "t", wq_tok[par])
                em.wait(nc.tensor, "t", q_ps_free)
                em.wait(nc.tensor, "t", xnq_done[blk])
                if blk == 0 and dsl == 0:
                    em.wait(nc.tensor, "t", ev_p0)
                    em.wait(nc.tensor, "t", ev_p0_ac)
                for h in range(16):
                    t_ = nc.tensor.matmul(q_ps[:][:, 0:512],
                                          wslq[par][:][:, h, :],
                                          xnq_fm[:, h, :],
                                          start=(h == 0), stop=(h == 15))
                ev_pe_q = em.inc(t_, "pe")
                if blk == 0:
                    tr_job(J1p[dsl], pl_ev)
                emit_kv_evac(blk, dsl, 0, ev_pe_h0)
                flush_norms(2)
                em.wait(nc.tensor, "t", kv_ps_free[1])
                for h in range(16):
                    for j in range(2):
                        c0 = 1024 + j * 512
                        t_ = nc.tensor.matmul(kv_ps[1][:][:, j * 512:(j + 1) * 512],
                                              wsl2[par][:][:, h, :],
                                              pn[:, h, c0:c0 + 512],
                                              start=(h == 0), stop=(h == 15))
                ev_pe_h1 = em.inc(t_, "pe")
                if dsl + 2 < 16:
                    em.wait(nc.gpsimd, "g", ev_pe_h1)
                    w_tok[par] = em.dma(nc.gpsimd.dma_start(
                        out=wsl2[par][:],
                        in_=wqkvT[:, 2048 + (dsl + 2) * 128:2048 + (dsl + 3) * 128]
                            .rearrange("(hc p) d -> p hc d", p=128)), f"wl{par}")
                    wq_tok[par] = em.dma(nc.gpsimd.dma_start(
                        out=wslq[par][:],
                        in_=wqkvT[:, (dsl + 2) * 128:(dsl + 3) * 128]
                            .rearrange("(hc p) d -> p hc d", p=128)), f"ql{par}")
                em.wait(nc.vector, "v", ev_pe_q)
                em.wait(nc.vector, "v", em.tok("p0"))
                em.wait(nc.vector, "v", q_dma)
                em.wait(nc.vector, "v", q_sq_done)
                ev_qst = em.inc(nc.vector.tensor_scalar(
                    qstage[:], q_ps[:], bq_cols[:, dsl:dsl + 1], None,
                    op0=ALU.add), "ve")
                q_ps_free = ev_qst
                em.wait(nc.sync, "y", ev_qst)
                q_dma = em.dma(nc.sync.dma_start(
                    out=q_b[dsl * 128:(dsl + 1) * 128, blk * 512:(blk + 1) * 512],
                    in_=qstage[:]), "qsb")
                while any(e[0] == 'q' and e[3] == par for e in norm_pending):
                    flush_norms(len(norm_pending) - 1)
                em.wait(nc.scalar, "s", ev_qst)
                em.wait(nc.scalar, "s", qsq_mm[par])
                ev_qsq = em.inc(nc.scalar.activation(qsq[par][:], qstage[:],
                                                     AF.Square), "ac")
                q_sq_done = ev_qsq
                norm_pending.append(('q', dsl, blk, par, ev_qsq))
                emit_kv_evac(blk, dsl, 1, ev_pe_h1)
                flush_norms(2)
            flush_norms(0)
            if blk == 0:
                preload_weights(False)
                run_jobs([("o", 1, t) for t in range(4)])

    ev_p2_pe = em.tok("pe")
    ev_p2_ve = em.tok("ve")
    ev_kv_done = [em.tok("ksb0"), em.tok("ksb1"), em.tok("vsb0"), em.tok("vsb1")]
    ev_q_done = em.tok("qsb")

    at_stack = ExitStack()
    at_sb = at_stack.enter_context(nc.sbuf_tensor("at_sb", [128, 16, 1024], BF16))
    p2kv_end = [ev_p2_pe, ev_p2_ve, em.tok("ac"), em.tok("po"), em.tok("ksb0"),
                em.tok("ksb1"), em.tok("vsb0"), em.tok("vsb1"), em.tok("qsb"),
                em.tok("xla0"), em.tok("xla1"), em.tok("xlb0"), em.tok("xlb1"),
                em.tok("wl0"), em.tok("wl1"), em.tok("ql0"), em.tok("ql1"),
                em.tok("p0")]
    phase_barrier(p2kv_end)
    d = nc.gpsimd.dma_start(out=at_sb[:],
                            in_=at_in.rearrange("(hc p) a -> p hc a", p=128))
    ev_at = em.dma(d, "wmisc")

    b_stack = ExitStack()
    bsb = lambda n, s_, d_: b_stack.enter_context(nc.sbuf_tensor(n, s_, d_))
    k_sb = [bsb(f"k_sb{i}", [128, 2, S], BF16) for i in range(2)]
    v_sb = [bsb(f"v_sb{i}", [128, 32, 256], BF16) for i in range(2)]
    q_sb = [bsb(f"q_sb{i}", [128, 2, OWN], BF16) for i in range(2)]
    NEXP = 3
    expb = [[bsb(f"expb{i}_{c}", [128, S], BF16) for c in range(2)] for i in range(NEXP)]
    NPAR = 3
    pcomb2 = [bsb(f"pcomb2_{i}", [128, S], BF16) for i in range(NPAR)]
    pt_sb = [bsb(f"pt_sb{i}", [128, 8, 128], BF16) for i in range(2)]
    sums = [bsb(f"sums{i}", [128, 16], F32) for i in range(3)]
    o_st = [bsb(f"o_st{i}", [128, 256], BF16) for i in range(2)]
    osq = [bsb("osq0", [128, 256], F32), bsb("osq1", [128, 256], F32)]
    osqq = [bsb("osqq0", [128, 256], F32), bsb("osqq1", [128, 256], F32)]
    brow = bsb("brow", [1, 128], F32)
    ps_s = [b_stack.enter_context(nc.psum_tensor(f"ps_s{i}", [128, 1024], F32))
            for i in range(2)]
    ps_t = [b_stack.enter_context(nc.psum_tensor(f"ps_t{i}", [128, 8, 128], BF16))
            for i in range(2)]
    ps_o = [b_stack.enter_context(nc.psum_tensor(f"ps_o{i}", [128, 256], F32))
            for i in range(2)]

    def emit_B_variant(c):
        P_MAP = [4 * c + r for r in range(4)] + [4 * (7 - c) + r for r in range(4)]

        def load_group(g_, par, after_pe):
            em.wait(nc.sync, "y", after_pe)
            em.wait(nc.sync, "y", ev_p2_pe)
            em.wait(nc.sync, "y", ev_p2_ve)
            for tk in ev_kv_done:
                em.wait(nc.sync, "y", tk)
            d1 = nc.sync.dma_start(out=k_sb[par][:],
                                   in_=k_b[g_ * 256:(g_ + 1) * 256, :]
                                   .rearrange("(cc p) t -> p cc t", p=128))
            tk1 = em.dma(d1, f"bg{par}")
            d2 = nc.sync.dma_start(out=v_sb[par][:],
                                   in_=v_b[:, g_ * 256:(g_ + 1) * 256]
                                   .rearrange("(tt p) e -> p tt e", p=128))
            tk2 = em.dma(d2, f"bgv{par}")
            return (tk1, tk2)

        def load_q(dh, par, after_pe):
            em.wait(nc.sync, "y", after_pe)
            em.wait(nc.sync, "y", ev_q_done)
            d1 = nc.sync.dma_start(out=q_sb[par][:],
                                   in_=q_b[dh * 256:(dh + 1) * 256, :]
                                   .rearrange("(cc p) t -> p cc t", p=128))
            return em.dma(d1, f"bq{par}")

        grp_tok = [None] * NG
        q_tok = [None] * NDH
        grp_tok[0] = load_group(0, 0, None)
        q_tok[0] = load_q(0, 0, None)
        grp_tok[1] = load_group(1, 1, None)
        q_tok[1] = load_q(1, 1, None)
        em.wait(nc.vector, "v", ev_p2_ve)
        for dh in range(NDH):
            g_ = dh // 2
            for cc in range(2):
                gc = g_ * 2 + cc
                for l in range(8):
                    p = P_MAP[l]
                    col = (dh * 2 + cc) * 8 + l
                    nc.vector.tensor_reduce(
                        brow[0:1, col:col + 1],
                        knmax[0:1, gc * 32:gc * 32 + p + 1],
                        axis=AX.X, op=ALU.max)
        nc.vector.drain()
        ev_v = em.inc(nc.vector.tensor_mul(brow[:], brow[:], qn2row[:]), "ve")
        em.wait(nc.scalar, "s", ev_v)
        nc.scalar.activation(brow[:], brow[:], AF.Ln, bias=eps_col[0:1, :])
        nc.scalar.drain()
        ev_a = em.inc(nc.scalar.activation(brow[:], brow[:], AF.Exp, scale=0.5), "ac")
        em.wait(nc.vector, "v", ev_a)
        ev_v = em.inc(nc.vector.tensor_scalar_mul(brow[:], brow[:], -float(SCALE)), "ve")
        em.wait(nc.tensor, "t", ev_v)
        em.wait(nc.tensor, "t", ev_p2_pe)
        t = nc.tensor.matmul(ps_o[0][:, 0:128], ones_row[:], brow[:],
                             start=True, stop=True)
        ev_pe = em.inc(t, "pe")
        em.wait(nc.vector, "v", ev_pe)
        ev_bound = em.inc(nc.vector.tensor_copy(bound_bc[:], ps_o[0][:, 0:128]), "ve")

        units = [(dh, l) for dh in range(NDH) for l in (0, 4, 1, 5, 2, 6, 3, 7)]
        NU = len(units)
        exp_done = {}
        comb_ev = {}
        pv_done = {}
        exp_ps_free = {}
        exp_free = {}
        pcomb_free = {}
        pt_free = [None, None]
        o_free = [ev_bound, None]
        o_dma = [None, None]
        o_st_done = [None, None]
        last_scores = {}

        def emit_scores(u):
            dh, l = units[u]
            par = u % NEXP
            spar = u % 3
            g_ = dh // 2
            gpar = g_ % 2
            qpar = dh % 2
            n = P_MAP[l] + 1
            nch = cdiv(n, 8)
            for _gt in grp_tok[g_]:
                em.wait(nc.tensor, "t", _gt)
            em.wait(nc.tensor, "t", q_tok[dh])
            ev_e = None
            for cc in range(2):
                for ch in range(nch):
                    t0 = ch * 8
                    sz = min(8, n - t0)
                    spp = (cc * nch + ch) % 2
                    em.wait(nc.tensor, "t", exp_ps_free.get(spp))
                    for j0 in range(0, sz, 4):
                        w = min(4, sz - j0) * 128
                        t = nc.tensor.matmul(
                            ps_s[spp][:, j0 * 128:j0 * 128 + w],
                            q_sb[qpar][:, cc, l * 128:(l + 1) * 128],
                            k_sb[gpar][:, cc, (t0 + j0) * 128:(t0 + j0) * 128 + w],
                            start=True, stop=True)
                    ev_pe = em.inc(t, "pe")
                    ev_ready = ev_pe
                    if ch == nch - 1:
                        em.wait(nc.vector, "v", ev_pe)
                        mm = nc.vector.tensor_tensor(
                            ps_s[spp][:, (sz - 1) * 128:sz * 128],
                            ps_s[spp][:, (sz - 1) * 128:sz * 128],
                            cmask[:], op=ALU.add)
                        ev_ready = em.inc(mm, "ve")
                    em.wait(nc.scalar, "s", ev_ready)
                    em.wait(nc.scalar, "s", exp_free.get(par))
                    em.wait(nc.scalar, "s", ev_bound)
                    col = (dh * 2 + cc) * 8 + l
                    ea = nc.scalar.activation(
                        expb[par][cc][:, t0 * 128:t0 * 128 + sz * 128],
                        ps_s[spp][:, 0:sz * 128],
                        AF.Exp, bias=bound_bc[:, col:col + 1], scale=float(SCALE),
                        accum_out=sums[spar][:, cc * 4 + ch:cc * 4 + ch + 1])
                    ev_e = em.inc(ea, "ac")
                    exp_ps_free[spp] = ev_e
            last_scores[dh] = em.tok("pe")
            return ev_e

        def emit_combine(u):
            dh, l = units[u]
            par = u % NEXP
            spar = u % 3
            n = P_MAP[l] + 1
            L = n * 128
            nch = cdiv(n, 8)
            em.wait(nc.vector, "v", exp_done[u])
            nc.vector.tensor_reduce(sums[spar][:, 8:9], sums[spar][:, 0:nch],
                                    axis=AX.X, op=ALU.add)
            nc.vector.tensor_reduce(sums[spar][:, 9:10], sums[spar][:, 4:4 + nch],
                                    axis=AX.X, op=ALU.add)
            nc.vector.drain()
            nc.vector.reciprocal(sums[spar][:, 10:11], sums[spar][:, 8:9])
            nc.vector.reciprocal(sums[spar][:, 11:12], sums[spar][:, 9:10])
            nc.vector.drain()
            nc.vector.tensor_mul(sums[spar][:, 12:13], sums[spar][:, 8:9],
                                 sums[spar][:, 11:12])
            nc.vector.drain()
            em.wait(nc.vector, "v", pcomb_free.get(u % NPAR))
            ev_a2 = em.inc(nc.vector.tensor_scalar(pcomb2[u % NPAR][:, 0:L],
                                                   expb[par][1][:, 0:L],
                                                   sums[spar][:, 12:13], lamneg[:],
                                                   op0=ALU.mult, op1=ALU.mult), "ve")
            em.wait(nc.gpsimd, "g", ev_a2)
            ins = nc.gpsimd.tensor_tensor(pcomb2[u % NPAR][:, 0:L],
                                          pcomb2[u % NPAR][:, 0:L],
                                          expb[par][0][:, 0:L], op=ALU.add)
            ev = em.inc(ins, "po")
            exp_free[par] = ev
            return ev

        def emit_pv(u):
            dh, l = units[u]
            par = u % NPAR
            g_ = dh // 2
            gpar = g_ % 2
            n = P_MAP[l] + 1
            opar = u % 2
            em.wait(nc.tensor, "t", comb_ev[u])
            em.wait(nc.tensor, "t", o_free[opar])
            first = True
            last_pe = None
            for t0 in range(0, n, 8):
                sz = min(8, n - t0)
                tp = (t0 // 8) % 2
                em.wait(nc.tensor, "t", pt_free[tp])
                for jj in range(sz):
                    tt = nc.tensor.transpose(
                        ps_t[tp][:, jj, :],
                        pcomb2[par][:, (t0 + jj) * 128:(t0 + jj + 1) * 128],
                        ident_bf[:])
                ev_pe = em.inc(tt, "pe")
                h1 = (sz + 1) // 2
                em.wait(nc.vector, "v", ev_pe)
                ev_v1 = em.inc(nc.vector.tensor_copy(pt_sb[tp][:, 0:h1, :],
                                                     ps_t[tp][:, 0:h1, :]), "ve")
                if sz > h1:
                    ev_v2 = em.inc(nc.vector.tensor_copy(pt_sb[tp][:, h1:sz, :],
                                                         ps_t[tp][:, h1:sz, :]), "ve")
                else:
                    ev_v2 = ev_v1
                em.wait(nc.tensor, "t", ev_v1)
                for jj in range(h1):
                    mm = nc.tensor.matmul(ps_o[opar][:, :], pt_sb[tp][:, jj, :],
                                          v_sb[gpar][:, t0 + jj, :],
                                          start=first, stop=(t0 + jj == n - 1))
                    first = False
                em.wait(nc.tensor, "t", ev_v2)
                for jj in range(h1, sz):
                    mm = nc.tensor.matmul(ps_o[opar][:, :], pt_sb[tp][:, jj, :],
                                          v_sb[gpar][:, t0 + jj, :],
                                          start=first, stop=(t0 + jj == n - 1))
                    first = False
                last_pe = em.inc(mm, "pe")
                pt_free[tp] = last_pe
            pcomb_free[u % NPAR] = last_pe
            return last_pe

        def emit_subln(u):
            dh, l = units[u]
            opar = u % 2
            spar = u % 3
            em.wait(nc.vector, "v", pv_done[u])
            em.wait(nc.vector, "v", o_st_done[opar])
            ev_c = em.inc(nc.vector.tensor_scalar(osq[opar][:], ps_o[opar][:, :],
                                                  sums[spar][:, 10:11], None,
                                                  op0=ALU.mult), "ve")
            em.wait(nc.scalar, "s", ev_c)
            nc.scalar.activation(osqq[opar][:], osq[opar][:], AF.Square,
                                 accum_out=sums[spar][:, 13:14])
            nc.scalar.drain()
            nc.scalar.activation(sums[spar][:, 14:15], sums[spar][:, 13:14], AF.Ln,
                                 bias=eps_col[:], scale=1.0 / 256)
            nc.scalar.drain()
            ev_a = em.inc(nc.scalar.activation(sums[spar][:, 15:16],
                                               sums[spar][:, 14:15], AF.Exp,
                                               scale=-0.5), "ac")
            em.wait(nc.vector, "v", ev_a)
            nc.vector.tensor_scalar(osq[opar][:], osq[opar][:], sums[spar][:, 15:16],
                                    None, op0=ALU.mult)
            nc.vector.drain()
            em.wait(nc.vector, "v", o_dma[opar])
            ev_v = em.inc(nc.vector.tensor_mul(o_st[opar][:], osq[opar][:], subln_sc[:]), "ve")
            o_st_done[opar] = ev_v
            em.wait(nc.sync, "y", ev_v)
            o_dma[opar] = em.dma(nc.sync.dma_start(
                out=o_b[l * 128:(l + 1) * 128, dh * 256:(dh + 1) * 256],
                in_=o_st[opar][:]), f"osb{opar}")
            return ev_c

        for u in range(NU):
            dh, l = units[u]
            exp_done[u] = emit_scores(u)
            if u >= 1:
                comb_ev[u - 1] = emit_combine(u - 1)
            if u >= 2:
                pv_done[u - 2] = emit_pv(u - 2)
            if u >= 3:
                o_free[(u - 3) % 2] = emit_subln(u - 3)
            if l == 1:
                g_ = dh // 2
                if dh % 2 == 0 and g_ + 1 < NG and grp_tok[g_ + 1] is None:
                    grp_tok[g_ + 1] = load_group(g_ + 1, (g_ + 1) % 2,
                                                 pv_done.get((2 * g_) * 8 - 1))
                if dh + 1 < NDH and q_tok[dh + 1] is None:
                    q_tok[dh + 1] = load_q(dh + 1, (dh + 1) % 2,
                                           last_scores.get(dh - 1))
        u = NU - 1
        comb_ev[u] = emit_combine(u)
        pv_done[u - 1] = emit_pv(u - 1)
        o_free[(u - 2) % 2] = emit_subln(u - 2)
        pv_done[u] = emit_pv(u)
        o_free[(u - 1) % 2] = emit_subln(u - 1)
        o_free[u % 2] = emit_subln(u)

    for _ch in ("bg0", "bg1", "bgv0", "bgv1", "bq0", "bq1", "osb0", "osb1"):
        em.ensure(_ch)
    pre_state = em.snap_state()
    ends = []
    for v in nc.Switch(engines=EV, index=[idxv] * len(EV), n=4):
        em.restore_state(pre_state)
        emit_B_variant(v)
        ends.append(dict(em.cnt))
    maxc = {q: max(e[q] for e in ends) for q in ends[0]}
    for qn in ("osb0", "osb1", "bg0", "bg1", "bgv0", "bgv1", "bq0", "bq1", "po"):
        assert all(e[qn] == ends[0][qn] for e in ends), qn
    eng_of = {"pe": nc.tensor, "ve": nc.vector, "ac": nc.scalar}
    for v in nc.Switch(engines=EV, index=[idxv] * len(EV), n=4):
        for qn in ("pe", "ve", "ac"):
            delta = maxc[qn] - ends[v][qn]
            if delta > 0:
                eng_of[qn].sem_inc(em.sems[qn], delta)
    em.restore_state(pre_state)
    em.cnt = dict(maxc)
    em.seen = {}
    b_stack.close()
    ev_B_pe = em.tok("pe")
    ev_B_ve = em.tok("ve")
    ev_o_done = [em.tok("osb0"), em.tok("osb1")]

    with ExitStack() as c12:
        cb = lambda n, s_, d_: c12.enter_context(nc.sbuf_tensor(n, s_, d_))
        xc = [cb("xc0", [128, HID], F32), cb("xc1", [128, HID], F32)]
        stat3 = cb("stat3", [128, 16], F32)
        xn_tm3 = cb("xn_tm3", [128, HID], BF16)
        xn_fm3 = cb("xn_fm3", [128, 16, 128], BF16)
        vsq = cb("vsq", [128, 1024], F32)
        arr2 = cb("arr2", [128, 64], F32)
        top8 = cb("top8", [128, 8], F32)
        zap = cb("zap", [128, 64], F32)
        routes = cb("routes", [128, 64], F32)
        vecs_bf = cb("vecs_bf", [128, 1024], BF16)
        vms = cb("vms", [128, 1024], F32)
        vmix = cb("vmix", [128, 16], F32)
        w_bf = cb("w_bf", [128, 1024], BF16)
        wfm_st = cb("wfm_st", [128, 8, 128], BF16)
        outw_sb = cb("outw_sb", [128, 16, HID], BF16)
        outbb = cb("outbb", [128, HID], F32)
        rowb = cb("rowb", [1, HID], F32)
        o_tm = [cb("o_tm0", [128, HID], BF16), cb("o_tm1", [128, HID], BF16)]
        o_fm = cb("o_fm", [128, 16, 128], BF16)
        s_scr = [cb("s_scr0", [128, HID], BF16), cb("s_scr1", [128, HID], BF16)]
        ps_vx = c12.enter_context(nc.psum_tensor("ps_vx", [128, 1024], F32))
        ps_s4 = c12.enter_context(nc.psum_tensor("ps_s4", [128, HID], F32))
        tr_ps3 = [c12.enter_context(nc.psum_tensor(f"tr_ps3{i}", [128, 4, 128], BF16)) for i in range(2)]

        phase_barrier([ev_B_pe, ev_B_ve, em.tok("ac"), em.tok("bg0"), em.tok("bg1"),
                       em.tok("bgv0"), em.tok("bgv1"),
                       em.tok("bq0"), em.tok("bq1")] + ev_o_done)
        em.wait(nc.gpsimd, "g", em.tok("wmisc"))
        outw_src = outwT_in.rearrange("(hc p) d -> p hc d", p=128)
        tok_w_q = [em.dma(nc.gpsimd.dma_start(
            out=outw_sb[:, qq * 4:(qq + 1) * 4, :],
            in_=outw_src[:, qq * 4:(qq + 1) * 4, :]), ch)
            for qq, ch in enumerate(("wmisc", "wq1", "wq2", "wq3"))]
        em.wait(nc.sync, "y", ev_B_pe)
        em.wait(nc.sync, "y", ev_B_ve)
        tok_b = em.dma(nc.sync.dma_start(out=rowb[:], in_=outb_in[:]), "rw2")
        em.wait(nc.tensor, "t", tok_b)
        ev_ob = emit_row_bcast(rowb, ps_vx[:, 0:512], outbb, HID)

        trp = TrPool(tr_ps3)
        ln_last = [None, None]
        x_tok = {}
        o_tok = {}
        vx_free = [ev_ob]
        vecs_mm = None
        s4_mm = None
        ps4_free = None
        w_read = None
        wf_dma = None
        epi_ve = None
        tr_pe3 = None
        s_dma = [None, None]
        o_consumed = [None, None]

        def issue_x(t):
            par = t % 2
            em.wait(nc.sync, "y", ln_last[par])
            x_tok[t] = em.dma(nc.sync.dma_start(
                out=xc[par][:], in_=x_own[t * 128:(t + 1) * 128, :]), f"xl{par}")

        def issue_o(t):
            par = t % 2
            em.wait(nc.sync, "y", o_consumed[par])
            for tk in ev_o_done:
                em.wait(nc.sync, "y", tk)
            o_tok[t] = em.dma(nc.sync.dma_start(
                out=o_tm[par][:], in_=o_b[t * 128:(t + 1) * 128, :]), f"ol{par}")

        def emit_c1_epilogue(t):
            nonlocal w_read, wf_dma, epi_ve, vx_free
            em.wait(nc.scalar, "s", vecs_mm)
            em.wait(nc.scalar, "s", epi_ve)
            ev_sqv = em.inc(nc.scalar.activation(vsq[:], ps_vx[:], AF.Square), "ac")
            em.wait(nc.vector, "v", ev_sqv)
            nc.vector.tensor_reduce(arr2[:], vsq[:].rearrange("p (a r) -> p a r", r=16),
                                    axis=AX.X, op=ALU.add)
            nc.vector.drain()
            nc.vector.max(out=top8[:], in_=arr2[:])
            nc.vector.drain()
            nc.vector.memset(top8[:, 4:8], -1.0)
            nc.vector.drain()
            nc.vector.match_replace(out=zap[:], in_to_replace=top8[:],
                                    in_values=arr2[:], imm_value=-1.0)
            nc.vector.drain()
            nc.vector.tensor_scalar(routes[:], zap[:], -1.0, 0.25,
                                    op0=ALU.is_le, op1=ALU.mult)
            ev_vec = em.inc(nc.vector.tensor_copy(vecs_bf[:], ps_vx[:]), "ve")
            vx_free = [ev_sqv, ev_vec]
            nc.vector.drain()
            nc.vector.tensor_tensor(
                vms[:].rearrange("p (r a) -> p r a", a=64),
                vecs_bf[:].rearrange("p (a r) -> p r a", r=16),
                routes[:].rearrange("p a -> p () a").to_broadcast([128, 16, 64]),
                op=ALU.mult)
            nc.vector.drain()
            nc.vector.tensor_reduce(vmix[:], vms[:].rearrange("p (r a) -> p r a", a=64),
                                    axis=AX.X, op=ALU.add)
            nc.vector.drain()
            em.wait(nc.vector, "v", w_read)
            ev_w = em.inc(nc.vector.tensor_tensor(
                w_bf[:].rearrange("p (a r) -> p a r", r=16),
                routes[:].to_broadcast([128, 64, 16]),
                vmix[:].rearrange("p r -> p () r").to_broadcast([128, 64, 16]),
                op=ALU.mult), "ve")
            em.wait(nc.tensor, "t", ev_w)
            ev_t = trp.run([w_bf[:, ar * 128:(ar + 1) * 128] for ar in range(8)],
                           [wfm_st[:, g0:g0 + 4, :] for g0 in range(0, 8, 4)],
                           extra_v_wait=wf_dma)
            w_read = em.tok("pe")
            epi_ve = em.tok("ve")
            em.wait(nc.sync, "y", ev_t)
            wf_dma = em.dma(nc.sync.dma_start(
                out=wfm_b.rearrange("(c p) t -> p c t", p=128)[:, :, t * 128:(t + 1) * 128],
                in_=wfm_st[:]), "wf")

        def emit_c2_epilogue(t):
            nonlocal ps4_free
            par = t % 2
            em.wait(nc.vector, "v", s4_mm)
            em.wait(nc.vector, "v", ev_ob)
            em.wait(nc.vector, "v", s_dma[par])
            ev_s = em.inc(nc.vector.tensor_add(s_scr[par][:], ps_s4[:, :], outbb[:]), "ve")
            ps4_free = ev_s
            em.wait(nc.sync, "y", ev_s)
            s_dma[par] = em.dma(nc.sync.dma_start(
                out=s_b[t * 128:(t + 1) * 128, :], in_=s_scr[par][:]), f"ssb{par}")

        issue_x(0)
        issue_o(0)
        issue_o(1)
        for t in range(8):
            par = t % 2
            em.wait(nc.scalar, "s", x_tok[t])
            em.wait(nc.vector, "v", x_tok[t])
            ev_v = emit_ln(xc[par][:], tr_pe3, stat3[:, par * 8:par * 8 + 8], xn_tm3)
            ln_last[par] = ev_v
            if t + 1 < 8:
                issue_x(t + 1)
            if t >= 1:
                emit_c1_epilogue(t - 1)
                emit_c2_epilogue(t - 1)
            em.wait(nc.tensor, "t", o_tok[t])
            ev2 = trp.run([o_tm[par][:][:, hc * 128:(hc + 1) * 128] for hc in range(16)],
                          [o_fm[:, g0:g0 + 4, :] for g0 in range(0, 16, 4)],
                          extra_v_wait=s4_mm)
            em.wait(nc.tensor, "t", ev2)
            for tkq in tok_w_q:
                em.wait(nc.tensor, "t", tkq)
            em.wait(nc.tensor, "t", ps4_free)
            for hc in range(16):
                for j in range(4):
                    mm = nc.tensor.matmul(ps_s4[:, j * 512:(j + 1) * 512],
                                          o_fm[:, hc, :],
                                          outw_sb[:, hc, j * 512:(j + 1) * 512],
                                          start=(hc == 0), stop=(hc == 15))
            ev_pe2 = em.inc(mm, "pe")
            s4_mm = ev_pe2
            o_consumed[par] = ev_pe2
            if t + 2 < 8:
                issue_o(t + 2)
            em.wait(nc.tensor, "t", ev_v)
            ev = trp.run([xn_tm3[:, hc * 128:(hc + 1) * 128] for hc in range(16)],
                         [xn_fm3[:, g0:g0 + 4, :] for g0 in range(0, 16, 4)],
                         extra_v_wait=vecs_mm)
            tr_pe3 = em.tok("pe")
            em.wait(nc.tensor, "t", ev)
            em.wait(nc.tensor, "t", ev_at)
            for tk in vx_free:
                em.wait(nc.tensor, "t", tk)
            for h in range(16):
                for j in range(2):
                    t2 = nc.tensor.matmul(ps_vx[:][:, j * 512:(j + 1) * 512],
                                          xn_fm3[:, h, :],
                                          at_sb[:, h, j * 512:(j + 1) * 512],
                                          start=(h == 0), stop=(h == 15))
            ev_pe = em.inc(t2, "pe")
            vecs_mm = ev_pe
        emit_c1_epilogue(7)
        emit_c2_epilogue(7)
    at_stack.close()
    ev_c2_pe = em.tok("pe")
    ev_c2_ve = em.tok("ve")
    ev_c2_ac = em.tok("ac")

    with ExitStack() as c3:
        cb = lambda n, s_, d_: c3.enter_context(nc.sbuf_tensor(n, s_, d_))
        wt_sb = cb("wt_sb", [128, 16, HID], BF16)
        b2_sb = cb("b2_sb", [128, 8, HID], BF16)
        ffnbb = cb("ffnbb", [128, HID], F32)
        xc2 = [cb("xc20", [128, HID], F32), cb("xc21", [128, HID], F32)]
        lnscr4 = cb("lnscr4", [128, HID], F32)
        stat4 = cb("stat4", [128, 16], F32)
        xn_tm4 = cb("xn_tm4", [128, HID], BF16)
        xn_fm4 = cb("xn_fm4", [128, 16, 128], BF16)
        wfm_ld = [cb("wfm_ld0", [128, 8, 128], BF16), cb("wfm_ld1", [128, 8, 128], BF16)]
        s_tile = [cb("s_tile0", [128, HID], BF16), cb("s_tile1", [128, HID], BF16)]
        g_scr = cb("g_scr", [128, HID], F32)
        sg_scr = cb("sg_scr", [128, HID], F32)
        y_st = [cb("y_st0", [128, HID], F32), cb("y_st1", [128, HID], F32)]
        bd = [c3.enter_context(nc.psum_tensor(f"ps_bd{i}", [128, 1024], F32))
              for i in range(2)]
        tr_ps5 = [c3.enter_context(nc.psum_tensor(f"tr_ps5{i}", [128, 4, 128], BF16)) for i in range(2)]
        p0w3 = c3.enter_context(nc.psum_tensor("p0w3", [128, 512], F32))

        phase_barrier([ev_c2_pe, ev_c2_ve, ev_c2_ac, em.tok("ol0"), em.tok("ol1"),
                       em.tok("ssb0"), em.tok("ssb1"), em.tok("rw2"),
                       em.tok("wmisc"), em.tok("wq1"), em.tok("wq2"),
                       em.tok("wq3"), em.tok("xl0"), em.tok("xl1"),
                       em.tok("wf"), em.tok("po")])
        em.wait(nc.gpsimd, "g", em.tok("wmisc"))
        wt_src = wt_in.rearrange("(hc p) d -> p hc d", p=128)
        tok_wt_q = [em.dma(nc.gpsimd.dma_start(
            out=wt_sb[:, qq * 4:(qq + 1) * 4, :],
            in_=wt_src[:, qq * 4:(qq + 1) * 4, :]), ch)
            for qq, ch in enumerate(("wmisc", "wq1", "wq2", "wq3"))]
        b2_src = b2_in.rearrange("(ac p) d -> p ac d", p=128)
        tok_b2_q = [em.dma(nc.gpsimd.dma_start(
            out=b2_sb[:, qq * 4:(qq + 1) * 4, :],
            in_=b2_src[:, qq * 4:(qq + 1) * 4, :]), ch)
            for qq, ch in enumerate(("wb1", "wb2"))]
        em.wait(nc.sync, "y", ev_c2_pe)
        em.wait(nc.sync, "y", ev_c2_ve)
        em.wait(nc.sync, "y", ev_c2_ac)
        tok_fb = em.dma(nc.sync.dma_start(out=lnscr4[0:1, :], in_=fb_in[:]), "rw3")
        em.wait(nc.tensor, "t", tok_fb)
        ev_fbb = emit_row_bcast(lnscr4, p0w3, ffnbb, HID)
        fbb_pe = em.tok("pe")

        trp = TrPool(tr_ps5)
        ln_last = [None, None]
        s_tok = {}
        x_tok = {}
        wf_tok = {}
        y_dma = [None, None]
        eps_free = [None, None]
        bd_mm_s = {}
        tr_pe5 = None
        tr_t = {}

        def issue_tile(t):
            par = t % 2
            em.wait(nc.sync, "y", ln_last[par])
            em.wait(nc.sync, "y", bd_mm_s.get(2 * t - 3))
            if t == 0:
                em.wait(nc.sync, "y", em.tok("wf"))
            wf_tok[t] = em.dma(nc.sync.dma_start(
                out=wfm_ld[par][:],
                in_=wfm_b.rearrange("(c p) t -> p c t", p=128)[:, :, t * 128:(t + 1) * 128]),
                f"wfl{par}")
            x_tok[t] = em.dma(nc.sync.dma_start(out=xc2[par][:],
                                                in_=x_own[t * 128:(t + 1) * 128, :]),
                              f"xl{par}")
            for tk in (em.tok("ssb0"), em.tok("ssb1")) if t < 2 else ():
                em.wait(nc.sync, "y", tk)
            s_tok[t] = em.dma(nc.sync.dma_start(
                out=s_tile[par][:], in_=s_b[t * 128:(t + 1) * 128, :]), f"sl{par}")

        def emit_c3_epilogue(s):
            t, hj = divmod(s, 2)
            par = t % 2
            b = s % 2
            cols = slice(hj * 1024, (hj + 1) * 1024)
            nc.vector.drain()
            em.wait(nc.vector, "v", bd_mm_s[s])
            em.wait(nc.vector, "v", ev_fbb)
            ev_bd = em.inc(nc.vector.tensor_add(sg_scr[:, cols], bd[b][:, :],
                                                ffnbb[:, cols]), "ve")
            eps_free[b] = ev_bd
            em.wait(nc.scalar, "s", ev_bd)
            ev_g = em.inc(nc.scalar.activation(g_scr[:, cols], sg_scr[:, cols], AF.Exp,
                                               scale=-1.0), "ac")
            em.wait(nc.vector, "v", ev_g)
            nc.vector.tensor_scalar_add(g_scr[:, cols], g_scr[:, cols], 1.0)
            nc.vector.drain()
            nc.vector.reciprocal(g_scr[:, cols], g_scr[:, cols])
            nc.vector.drain()
            nc.vector.tensor_mul(g_scr[:, cols], g_scr[:, cols], sg_scr[:, cols])
            nc.vector.drain()
            em.wait(nc.vector, "v", s_tok[t])
            nc.vector.tensor_mul(sg_scr[:, cols], s_tile[par][:, cols], g_scr[:, cols])
            nc.vector.drain()
            em.wait(nc.vector, "v", y_dma[par])
            ev_y = em.inc(nc.vector.tensor_add(y_st[par][:, cols], sg_scr[:, cols],
                                               xc2[par][:, cols]), "ve")
            if hj == 1:
                ln_last[par] = ev_y
            em.wait(nc.sync, "y", ev_y)
            y_dma[par] = em.dma(nc.sync.dma_start(
                out=y_out[t * 128:(t + 1) * 128, cols], in_=y_st[par][:, cols]),
                f"yb{par}")

        issue_tile(0)
        issue_tile(1)
        for s in range(16):
            t, hj = divmod(s, 2)
            par = t % 2
            b = s % 2
            if hj == 0:
                em.wait(nc.scalar, "s", x_tok[t])
                em.wait(nc.scalar, "s", fbb_pe)
                em.wait(nc.vector, "v", x_tok[t])
                ev_v = emit_ln(xc2[par][:], tr_pe5, stat4[:, par * 8:par * 8 + 8],
                               xn_tm4)
                em.wait(nc.tensor, "t", ev_v)
                ev = trp.run([xn_tm4[:, hc * 128:(hc + 1) * 128] for hc in range(16)],
                             [xn_fm4[:, g0:g0 + 4, :] for g0 in range(0, 16, 4)],
                             extra_v_wait=bd_mm_s.get(2 * t - 1))
                tr_pe5 = em.tok("pe")
                tr_t[t] = ev
            em.wait(nc.tensor, "t", tr_t[t])
            for tkq in tok_wt_q:
                em.wait(nc.tensor, "t", tkq)
            for tkq in tok_b2_q:
                em.wait(nc.tensor, "t", tkq)
            em.wait(nc.tensor, "t", eps_free[b])
            for hc in range(16):
                for j in range(2):
                    mm = nc.tensor.matmul(bd[b][:, j * 512:(j + 1) * 512],
                                          xn_fm4[:, hc, :],
                                          wt_sb[:, hc, hj * 1024 + j * 512:hj * 1024 + (j + 1) * 512],
                                          start=(hc == 0), stop=False)
            em.wait(nc.tensor, "t", wf_tok[t])
            for ar in range(8):
                for j in range(2):
                    mm = nc.tensor.matmul(bd[b][:, j * 512:(j + 1) * 512],
                                          wfm_ld[par][:][:, ar, :],
                                          b2_sb[:, ar, hj * 1024 + j * 512:hj * 1024 + (j + 1) * 512],
                                          start=False, stop=(ar == 7))
            bd_mm_s[s] = em.inc(mm, "pe")
            if s >= 1:
                emit_c3_epilogue(s - 1)
                if s % 2 == 0 and s // 2 + 1 <= 7:
                    issue_tile(s // 2 + 1)
        emit_c3_epilogue(15)
        em.wait(nc.sync, "y", em.tok("yb0"))
        em.wait(nc.sync, "y", em.tok("yb1"))

    stack.close()
    return nc


_NC_CACHE = [None]


def _prep(inputs):
    hid = np.asarray(inputs["L0_attn_hidden_states"], dtype=np.float32)
    wqkvT = np.ascontiguousarray(np.asarray(inputs["L0_attn_Wqkv_w"], np.float32).T)
    bq = np.ascontiguousarray(np.asarray(inputs["L0_attn_Wqkv_b"], np.float32).reshape(1, 4096))
    lrows = np.ascontiguousarray(np.stack([
        np.asarray(inputs["L0_attn_lambda_q1"], np.float32),
        np.asarray(inputs["L0_attn_lambda_k1"], np.float32),
        np.asarray(inputs["L0_attn_lambda_q2"], np.float32),
        np.asarray(inputs["L0_attn_lambda_k2"], np.float32)]))
    subln = np.ascontiguousarray(np.asarray(inputs["L0_attn_subln_w"], np.float32).reshape(1, 256))
    outwT = np.ascontiguousarray(np.asarray(inputs["L0_attn_out_w"], np.float32).T)
    outb = np.ascontiguousarray(np.asarray(inputs["L0_attn_out_b"], np.float32).reshape(1, 2048))
    A = np.asarray(inputs["L0_ffn_A"], np.float32)
    Bm = np.asarray(inputs["L0_ffn_B"], np.float32)
    Wf = np.asarray(inputs["L0_ffn_W"], np.float32)
    fb = np.ascontiguousarray(np.asarray(inputs["L0_ffn_bias"], np.float32).reshape(1, 2048))
    a_t = np.ascontiguousarray(A.reshape(1024, 2048).T)
    b2 = np.ascontiguousarray(Bm.transpose(0, 2, 1).reshape(1024, 2048))
    w_t = np.ascontiguousarray(Wf.T)

    in_maps = []
    rows_per_core = []
    for i in range(8):
        b = i // 4
        c = i % 4
        rows = np.r_[512 * c:512 * (c + 1), 512 * (7 - c):512 * (8 - c)]
        rows_per_core.append((b, rows))
        in_maps.append({
            "x_full": np.ascontiguousarray(hid[b]),
            "x_own": np.ascontiguousarray(hid[b][rows]),
            "wqkvT": wqkvT, "bqkv": bq, "lrows": lrows, "subln": subln,
            "outwT": outwT, "outb": outb, "a_t": a_t, "b2": b2,
            "w_t": w_t, "fbias": fb,
        })
    return in_maps, rows_per_core


def kernel(**inputs):
    in_maps, rows_per_core = _prep(inputs)
    if _NC_CACHE[0] is None:
        _NC_CACHE[0] = build_program()
    nc = _NC_CACHE[0]
    res = run_bass_kernel_spmd(nc, in_maps, core_ids=list(range(8)))
    out = np.zeros((2, 4096, 2048), np.float32)
    for i in range(8):
        b, rows = rows_per_core[i]
        out[b][rows] = np.asarray(res.results[i]["y"])
    return out


if __name__ == "__main__":
    import time
    t0 = time.time()
    nc = build_program()
    print("built instructions:", nc.n_instructions(), "in", round(time.time() - t0, 1), "s")


# version 26
# speedup vs baseline: 1.2176x; 1.2176x over previous
import math
import numpy as np
from contextlib import ExitStack

import concourse.bass as bass
import concourse.mybir as mybir
from concourse.bass_utils import run_bass_kernel_spmd

F32 = mybir.dt.float32
BF16 = mybir.dt.bfloat16
AF = mybir.ActivationFunctionType
ALU = mybir.AluOpType
AX = mybir.AxisListType

S = 4096
HID = 2048
OWN = 1024
NDH = 8
NG = 4
SCALE = 128.0 ** -0.5
LI = 0.8 - 0.6 * math.exp(-0.3)
EPS = 1e-5
MASKVAL = -30000.0


def cdiv(a, b):
    return (a + b - 1) // b


class Em:
    def __init__(self, nc, stack):
        self.nc = nc
        self.stack = stack
        self.sems = {}
        self.cnt = {}
        self.step = {}
        for name, step in [("pe", 1), ("ve", 1), ("ac", 1), ("po", 1)]:
            self.sems[name] = stack.enter_context(nc.semaphore(name))
            self.cnt[name] = 0
            self.step[name] = step
        self.seen = {}

    def ensure(self, ch):
        if ch not in self.sems:
            self.sems[ch] = self.stack.enter_context(self.nc.semaphore(ch))
            self.cnt[ch] = 0
            self.step[ch] = 16

    def dma(self, instr, ch):
        if ch not in self.sems:
            self.sems[ch] = self.stack.enter_context(self.nc.semaphore(ch))
            self.cnt[ch] = 0
            self.step[ch] = 16
        return self.inc(instr, ch)

    def inc(self, instr, q, k=1):
        v = self.cnt[q] + self.step[q] * k
        self.cnt[q] = v
        instr.then_inc(self.sems[q], self.step[q] * k)
        return (q, v)

    def tok(self, q):
        return (q, self.cnt[q])

    def wait(self, eng, ename, tok):
        if tok is None:
            return
        q, v = tok
        if v <= 0:
            return
        key = (ename, q)
        if self.seen.get(key, 0) >= v:
            return
        eng.wait_ge(self.sems[q], v)
        self.seen[key] = v

    def snap_state(self):
        return (dict(self.cnt), dict(self.seen))

    def restore_state(self, st):
        self.cnt = dict(st[0])
        self.seen = dict(st[1])


def build_program():
    nc = bass.Bass("TRN2", target_bir_lowering=False, debug=False)

    x_full = nc.dram_tensor("x_full", [S, HID], F32, kind="ExternalInput").ap()
    x_own = nc.dram_tensor("x_own", [OWN, HID], F32, kind="ExternalInput").ap()
    wqkvT = nc.dram_tensor("wqkvT", [HID, 4096], F32, kind="ExternalInput").ap()
    bqkv = nc.dram_tensor("bqkv", [1, 4096], F32, kind="ExternalInput").ap()
    lrow_in = nc.dram_tensor("lrows", [4, 128], F32, kind="ExternalInput").ap()
    subln_in = nc.dram_tensor("subln", [1, 256], F32, kind="ExternalInput").ap()
    outwT_in = nc.dram_tensor("outwT", [HID, HID], F32, kind="ExternalInput").ap()
    outb_in = nc.dram_tensor("outb", [1, HID], F32, kind="ExternalInput").ap()
    at_in = nc.dram_tensor("a_t", [HID, 1024], F32, kind="ExternalInput").ap()
    b2_in = nc.dram_tensor("b2", [1024, HID], F32, kind="ExternalInput").ap()
    wt_in = nc.dram_tensor("w_t", [HID, HID], F32, kind="ExternalInput").ap()
    fb_in = nc.dram_tensor("fbias", [1, HID], F32, kind="ExternalInput").ap()
    y_out = nc.dram_tensor("y", [OWN, HID], F32, kind="ExternalOutput").ap()

    k_b = nc.dram_tensor("k_b", [1024, S], BF16).ap()
    v_b = nc.dram_tensor("v_b", [S, 1024], BF16).ap()
    q_b = nc.dram_tensor("q_b", [2048, OWN], BF16).ap()
    o_b = nc.dram_tensor("o_b", [OWN, 2048], BF16).ap()
    s_b = nc.dram_tensor("s_b", [OWN, HID], BF16).ap()
    wfm_b = nc.dram_tensor("wfm_b", [1024, OWN], BF16).ap()

    stack = ExitStack()
    sb = lambda n, s_, d: stack.enter_context(nc.sbuf_tensor(n, s_, d))
    em = Em(nc, stack)
    EV = [nc.tensor, nc.vector, nc.scalar, nc.sync, nc.gpsimd]

    ident_bf = sb("ident_bf", [128, 128], BF16)
    cmask = sb("cmask", [128, 128], BF16)
    ones_row = sb("ones_row", [1, 128], F32)
    ones_col_bf = sb("ones_col_bf", [128, 1], BF16)
    bq_cols = sb("bq_cols", [128, 32], F32)
    lamneg = sb("lamneg", [128, 1], F32)
    subln_sc = sb("subln_sc", [128, 256], F32)
    qn2row = sb("qn2row", [1, 128], F32)
    knmax = sb("knmax", [1, 256], F32)
    bound_bc = sb("bound_bc", [128, 128], F32)
    eps_col = sb("eps_col", [128, 1], F32)

    def emit_ln(x_tile, scr, stat, out_bf):
        em.wait(nc.scalar, "s", scr)
        nc.scalar.activation(out_bf[:], x_tile, AF.Copy, accum_out=stat[:, 0:1])
        nc.scalar.drain()
        ev_a = em.inc(nc.scalar.activation(out_bf[:], x_tile, AF.Square,
                                           accum_out=stat[:, 1:2]), "ac")
        em.wait(nc.vector, "v", ev_a)
        nc.vector.tensor_scalar_mul(stat[:, 2:3], stat[:, 0:1], 1.0 / HID)
        nc.vector.tensor_scalar_mul(stat[:, 3:4], stat[:, 1:2], 1.0 / HID)
        nc.vector.drain()
        nc.vector.tensor_mul(stat[:, 4:5], stat[:, 2:3], stat[:, 2:3])
        nc.vector.drain()
        ev_v = em.inc(nc.vector.tensor_sub(stat[:, 5:6], stat[:, 3:4], stat[:, 4:5]), "ve")
        em.wait(nc.scalar, "s", ev_v)
        nc.scalar.activation(stat[:, 6:7], stat[:, 5:6], AF.Ln, bias=eps_col[:])
        nc.scalar.drain()
        ev_a = em.inc(nc.scalar.activation(stat[:, 7:8], stat[:, 6:7], AF.Exp,
                                           scale=-0.5), "ac")
        em.wait(nc.vector, "v", ev_a)
        nc.vector.tensor_mul(stat[:, 4:5], stat[:, 2:3], stat[:, 7:8])
        nc.vector.drain()
        nc.vector.tensor_scalar_mul(stat[:, 4:5], stat[:, 4:5], -1.0)
        nc.vector.drain()
        return em.inc(nc.vector.tensor_scalar(out_bf[:], x_tile, stat[:, 7:8],
                                              stat[:, 4:5], op0=ALU.mult,
                                              op1=ALU.add), "ve")

    class TrPool:

        def __init__(self, ps2):
            self.ps = ps2
            self.free = [None, None]
            self.i = 0

        def run(self, srcs, dsts, extra_v_wait=None):
            ev = None
            gi = 0
            for g0 in range(0, len(srcs), 4):
                grp = srcs[g0:g0 + 4]
                bank = self.i % 2
                self.i += 1
                em.wait(nc.tensor, "t", self.free[bank])
                for j, s_ in enumerate(grp):
                    tt = nc.tensor.transpose(self.ps[bank][:][:, j, :], s_, ident_bf[:])
                ev_pe = em.inc(tt, "pe")
                em.wait(nc.vector, "v", ev_pe)
                if extra_v_wait is not None:
                    em.wait(nc.vector, "v", extra_v_wait)
                ev = em.inc(nc.vector.tensor_copy(dsts[gi],
                                                  self.ps[bank][:][:, 0:len(grp), :]), "ve")
                self.free[bank] = ev
                gi += 1
            return ev

    def emit_row_bcast(row_ap, psum, dst, ncols, scale=None):
        ev = None
        for j0 in range(0, ncols, 512):
            w = min(512, ncols - j0)
            em.wait(nc.tensor, "t", em.tok("ve"))
            t = nc.tensor.matmul(psum[:, 0:w], ones_row[:], row_ap[0:1, j0:j0 + w],
                                 start=True, stop=True)
            ev_pe = em.inc(t, "pe")
            em.wait(nc.vector, "v", ev_pe)
            if scale is None:
                ins = nc.vector.tensor_copy(dst[:, j0:j0 + w], psum[:, 0:w])
            else:
                ins = nc.vector.tensor_scalar_mul(dst[:, j0:j0 + w], psum[:, 0:w], scale)
            ev = em.inc(ins, "ve")
        return ev

    def phase_barrier(tokens):
        for eng, en in [(nc.sync, "y"), (nc.gpsimd, "g"), (nc.vector, "v"),
                        (nc.scalar, "s"), (nc.tensor, "t")]:
            for tk in tokens:
                em.wait(eng, en, tk)

    g = nc.gpsimd
    g.memset(ident_bf[:], 0.0)
    g.memset(cmask[:], 0.0)
    g.memset(ones_row[:], 1.0)
    g.memset(ones_col_bf[:], 1.0)
    g.memset(eps_col[:], float(EPS))
    g.drain()
    g.affine_select(out=ident_bf[:], in_=ident_bf[:], compare_op=ALU.not_equal,
                    fill=1.0, base=0, pattern=[[-1, 128]], channel_multiplier=1)
    tm = g.affine_select(out=cmask[:], in_=cmask[:], compare_op=ALU.is_ge,
                         fill=MASKVAL, base=0, pattern=[[-1, 128]],
                         channel_multiplier=1)
    ev_const = em.inc(tm, "po")

    pid = nc.partition_id(engines=[e.engine for e in EV])
    idxv = nc.snap(pid % 4, engines=[e.engine for e in EV])

    with ExitStack() as p2kv:
        lcols = p2kv.enter_context(nc.sbuf_tensor("lcols", [128, 4], F32))
        dots = p2kv.enter_context(nc.sbuf_tensor("dots", [1, 4], F32))
        lam_t = p2kv.enter_context(nc.sbuf_tensor("lam_t", [1, 4], F32))
        xk0 = p2kv.enter_context(nc.sbuf_tensor("xk0", [128, HID], F32))
        xk1 = p2kv.enter_context(nc.sbuf_tensor("xk1", [128, HID], F32))
        stat2 = p2kv.enter_context(nc.sbuf_tensor("stat2", [128, 16], F32))
        xn_tm2 = p2kv.enter_context(nc.sbuf_tensor("xn_tm2", [128, HID], BF16))
        xn_tm2_b = p2kv.enter_context(nc.sbuf_tensor("xn_tm2_b", [128, HID], BF16))
        panelA = p2kv.enter_context(nc.sbuf_tensor("panelA", [128, 16, 2048], BF16))
        panelB = p2kv.enter_context(nc.sbuf_tensor("panelB", [128, 16, 2048], BF16))
        xnq_fm = p2kv.enter_context(nc.sbuf_tensor("xnq_fm", [128, 16, 512], BF16))
        wsl20 = p2kv.enter_context(nc.sbuf_tensor("wsl20", [128, 16, 128], BF16))
        wsl21 = p2kv.enter_context(nc.sbuf_tensor("wsl21", [128, 16, 128], BF16))
        wslq0 = p2kv.enter_context(nc.sbuf_tensor("wslq0", [128, 16, 128], BF16))
        wslq1 = p2kv.enter_context(nc.sbuf_tensor("wslq1", [128, 16, 128], BF16))
        kvstage0 = p2kv.enter_context(nc.sbuf_tensor("kvstage0", [128, 1024], BF16))
        kvstage1 = p2kv.enter_context(nc.sbuf_tensor("kvstage1", [128, 1024], BF16))
        ksqA = p2kv.enter_context(nc.sbuf_tensor("ksqA", [128, 1024], BF16))
        ksqB = p2kv.enter_context(nc.sbuf_tensor("ksqB", [128, 1024], BF16))
        vt0 = p2kv.enter_context(nc.sbuf_tensor("vt0", [128, 8, 128], BF16))
        vt1 = p2kv.enter_context(nc.sbuf_tensor("vt1", [128, 8, 128], BF16))
        qstage = p2kv.enter_context(nc.sbuf_tensor("qstage", [128, 512], BF16))
        qsqA = p2kv.enter_context(nc.sbuf_tensor("qsqA", [128, 512], BF16))
        qsqB = p2kv.enter_context(nc.sbuf_tensor("qsqB", [128, 512], BF16))
        tr_ps20 = p2kv.enter_context(nc.psum_tensor("tr_ps20", [128, 4, 128], BF16))
        tr_ps21 = p2kv.enter_context(nc.psum_tensor("tr_ps21", [128, 4, 128], BF16))
        kv_ps0 = p2kv.enter_context(nc.psum_tensor("kv_ps0", [128, 1024], F32))
        kv_ps1 = p2kv.enter_context(nc.psum_tensor("kv_ps1", [128, 1024], F32))
        n_ps20 = p2kv.enter_context(nc.psum_tensor("n_ps20", [1, 512], F32))
        p0_sub = ExitStack()
        p0_ps = p0_sub.enter_context(nc.psum_tensor("p0_ps", [128, 256], F32))
        xk = [xk0, xk1]; wsl2 = [wsl20, wsl21]; wslq = [wslq0, wslq1]
        kvstage = [kvstage0, kvstage1]; ksq = [ksqA, ksqB]; qsq = [qsqA, qsqB]
        vt = [vt0, vt1]
        tr_ps2 = [tr_ps20, tr_ps21]; kv_ps = [kv_ps0, kv_ps1]
        panel = [panelA, panelB]
        xn_tm2b = [xn_tm2, xn_tm2_b]

        em.wait(nc.tensor, "t", ev_const)
        em.wait(nc.scalar, "s", ev_const)
        em.wait(nc.vector, "v", ev_const)
        trp = TrPool(tr_ps2)

        ln_last = [None, None]
        tr_pe2 = [None, None]
        panel_done = [None, None]
        xnq_done = [None, None]
        job_tok = {}
        job_par = {}
        jseq = [0]

        def issue_job(job):
            par = jseq[0] % 2
            jseq[0] += 1
            job_par[job] = par
            em.wait(nc.sync, "y", ln_last[par])
            kind, blk_, t = job
            if kind == "p":
                src, r0 = x_full, (blk_ * 16 + t) * 128
            else:
                src, r0 = x_own, blk_ * 512 + t * 128
            tka = em.dma(nc.sync.dma_start(
                out=xk[par][:, 0:1024], in_=src[r0:r0 + 128, 0:1024]), f"xla{par}")
            tkb = em.dma(nc.sync.dma_start(
                out=xk[par][:, 1024:2048], in_=src[r0:r0 + 128, 1024:2048]), f"xlb{par}")
            job_tok[job] = (tka, tkb)

        def ln_job(job):
            par = job_par[job]
            for tk in job_tok[job]:
                em.wait(nc.scalar, "s", tk)
                em.wait(nc.vector, "v", tk)
            ev_v = emit_ln(xk[par][:], tr_pe2[par], stat2[:, par * 8:par * 8 + 8],
                           xn_tm2b[par])
            ln_last[par] = ev_v
            return ev_v

        def tr_job(job, ev_v):
            par = job_par[job]
            kind, blk_, t = job
            em.wait(nc.tensor, "t", ev_v)
            if kind == "p":
                dsts = [panel[blk_ % 2][:, g0:g0 + 4, t * 128:(t + 1) * 128]
                        for g0 in range(0, 16, 4)]
            else:
                dsts = [xnq_fm[:, g0:g0 + 4, t * 128:(t + 1) * 128]
                        for g0 in range(0, 16, 4)]
            ev = trp.run([xn_tm2b[par][:, hc * 128:(hc + 1) * 128] for hc in range(16)],
                         dsts)
            tr_pe2[par] = em.tok("pe")
            if kind == "p":
                panel_done[blk_ % 2] = ev
            else:
                xnq_done[blk_] = ev
            return ev

        def run_jobs(jobs):
            for j in jobs[:2]:
                issue_job(j)
            prev = None
            for i, j in enumerate(jobs):
                ev_v = ln_job(j)
                if i + 2 < len(jobs):
                    issue_job(jobs[i + 2])
                if prev is not None:
                    tr_job(*prev)
                prev = (j, ev_v)
            tr_job(*prev)

        em.dma(nc.sync.dma_start(
            out=bq_cols[:], in_=bqkv.rearrange("o (j p) -> p (o j)", p=128),
            allow_slow_non_contiguous=True), "p0")
        em.dma(nc.sync.dma_start(
            out=lcols[:], in_=lrow_in.rearrange("r c -> c r"),
            allow_slow_non_contiguous=True), "p0")
        em.dma(nc.sync.dma_start(out=subln_sc[0:1, :], in_=subln_in[:]), "p0")
        tok_p0 = em.tok("p0")
        em.wait(nc.tensor, "t", tok_p0)
        nc.tensor.matmul(p0_ps[0:1, 250:251], lcols[:, 0:1], lcols[:, 1:2],
                         start=True, stop=True)
        t_ = nc.tensor.matmul(p0_ps[0:1, 251:252], lcols[:, 2:3], lcols[:, 3:4],
                              start=True, stop=True)
        ev_pe = em.inc(t_, "pe")
        em.wait(nc.scalar, "s", ev_pe)
        em.inc(nc.scalar.activation(dots[:, 0:2], p0_ps[0:1, 250:252], AF.Exp), "ac")
        ev_p0_ac = em.tok("ac")
        em.wait(nc.vector, "v", ev_p0_ac)
        nc.vector.tensor_sub(lam_t[:, 0:1], dots[:, 0:1], dots[:, 1:2])
        nc.vector.drain()
        ev_lam = em.inc(nc.vector.tensor_scalar(lam_t[:, 1:2], lam_t[:, 0:1],
                                                float(LI), -1.0,
                                                op0=ALU.add, op1=ALU.mult), "ve")
        em.wait(nc.tensor, "t", ev_lam)
        t_ = nc.tensor.matmul(p0_ps[:, 36:37], ones_row[:], lam_t[:, 1:2],
                              start=True, stop=True)
        ev_pe = em.inc(t_, "pe")
        em.wait(nc.vector, "v", ev_pe)
        ev_lnb = em.inc(nc.vector.tensor_copy(lamneg[:], p0_ps[:, 36:37]), "ve")
        em.wait(nc.tensor, "t", ev_lnb)
        t_ = nc.tensor.matmul(p0_ps[:, 0:256], ones_row[:], subln_sc[0:1, :],
                              start=True, stop=True)
        ev_pe = em.inc(t_, "pe")
        em.wait(nc.vector, "v", ev_pe)
        em.inc(nc.vector.tensor_scalar_mul(subln_sc[:], p0_ps[:, 0:256],
                                           float(1.0 - LI)), "ve")
        ev_p0 = em.tok("ve")
        p0_sub.close()
        q_ps = p2kv.enter_context(nc.psum_tensor("q_ps", [128, 512], F32))

        w_tok = [None, None]
        wq_tok = [None, None]
        kv_ps_free = [None, None]
        q_ps_free = None
        k_dma = [None, None]
        k_sq_done = [None, None]
        ksq_mm = [None, None]
        q_dma = None
        q_sq_done = None
        qsq_mm = [None, None]
        nrm_free = None
        vt_dma = [None, None]
        v_tr_done = [None, None]
        norm_pending = []

        def flush_norms(keep):
            nonlocal nrm_free
            while len(norm_pending) > keep:
                ent = norm_pending.pop(0)
                if ent[0] == 'k':
                    _, dsl0, half0, blk0, ev_sq0 = ent
                    for j in range(2):
                        em.wait(nc.tensor, "t", ev_sq0)
                        em.wait(nc.tensor, "t", nrm_free)
                        t0 = nc.tensor.matmul(n_ps20[0:1, :], ones_col_bf[:],
                                              ksq[half0][:, j * 512:(j + 1) * 512],
                                              start=True, stop=True)
                        ev_pe0 = em.inc(t0, "pe")
                        em.wait(nc.vector, "v", ev_pe0)
                        blkpos = blk0 * 16 + half0 * 8 + j * 4
                        nrm_free = em.inc(nc.vector.tensor_reduce(
                            knmax[0:1, dsl0 * 32 + blkpos:dsl0 * 32 + blkpos + 4],
                            n_ps20[0:1, :].rearrange("o (a b) -> o a b", b=128),
                            axis=AX.X, op=ALU.max), "ve")
                    ksq_mm[half0] = em.tok("pe")
                else:
                    _, dsl0, blk0, par0, ev_sq0 = ent
                    em.wait(nc.tensor, "t", ev_sq0)
                    em.wait(nc.tensor, "t", nrm_free)
                    t0 = nc.tensor.matmul(n_ps20[0:1, :], ones_col_bf[:],
                                          qsq[par0][:, :], start=True, stop=True)
                    ev_pe0 = em.inc(t0, "pe")
                    em.wait(nc.vector, "v", ev_pe0)
                    nrm_free = em.inc(nc.vector.tensor_reduce(
                        qn2row[0:1, dsl0 * 8 + blk0 * 4:dsl0 * 8 + blk0 * 4 + 4],
                        n_ps20[0:1, :].rearrange("o (a b) -> o a b", b=128),
                        axis=AX.X, op=ALU.max), "ve")
                    qsq_mm[par0] = em.tok("pe")

        def preload_weights(first):
            if not first:
                em.wait(nc.gpsimd, "g", em.tok("pe"))
            for d2 in range(2):
                w_tok[d2] = em.dma(nc.gpsimd.dma_start(
                    out=wsl2[d2][:],
                    in_=wqkvT[:, 2048 + d2 * 128:2048 + (d2 + 1) * 128]
                        .rearrange("(hc p) d -> p hc d", p=128)), f"wl{d2}")
                wq_tok[d2] = em.dma(nc.gpsimd.dma_start(
                    out=wslq[d2][:],
                    in_=wqkvT[:, d2 * 128:(d2 + 1) * 128]
                        .rearrange("(hc p) d -> p hc d", p=128)), f"ql{d2}")

        def emit_kv_evac(blk, dsl, half, ev_pe_mm):
            em.wait(nc.vector, "v", ev_pe_mm)
            em.wait(nc.vector, "v", em.tok("p0"))
            em.wait(nc.vector, "v", k_dma[half])
            em.wait(nc.vector, "v", k_sq_done[half])
            em.wait(nc.vector, "v", v_tr_done[half])
            ev_st = em.inc(nc.vector.tensor_scalar(
                kvstage[half][:], kv_ps[half][:], bq_cols[:, 16 + dsl:17 + dsl],
                None, op0=ALU.add), "ve")
            kv_ps_free[half] = ev_st
            if dsl < 8:
                em.wait(nc.sync, "y", ev_st)
                k_dma[half] = em.dma(nc.sync.dma_start(
                    out=k_b[dsl * 128:(dsl + 1) * 128,
                            blk * 2048 + half * 1024:blk * 2048 + half * 1024 + 1024],
                    in_=kvstage[half][:]), f"ksb{half}")
                while any(e[0] == 'k' and e[2] == half for e in norm_pending):
                    flush_norms(len(norm_pending) - 1)
                em.wait(nc.scalar, "s", ev_st)
                em.wait(nc.scalar, "s", ksq_mm[half])
                ev_sq = em.inc(nc.scalar.activation(ksq[half][:], kvstage[half][:],
                                                    AF.Square), "ac")
                k_sq_done[half] = ev_sq
                norm_pending.append(('k', dsl, half, blk, ev_sq))
            else:
                em.wait(nc.tensor, "t", ev_st)
                ev_tr = trp.run([kvstage[half][:, tt * 128:(tt + 1) * 128]
                                 for tt in range(8)],
                                [vt[half][:, g0:g0 + 4, :] for g0 in range(0, 8, 4)],
                                extra_v_wait=vt_dma[half])
                v_tr_done[half] = em.tok("pe")
                em.wait(nc.sync, "y", ev_tr)
                vt_dma[half] = em.dma(nc.sync.dma_start(
                    out=v_b.rearrange("(a p) e -> p a e", p=128)
                        [:, blk * 16 + half * 8:blk * 16 + half * 8 + 8,
                         (dsl - 8) * 128:(dsl - 8) * 128 + 128],
                    in_=vt[half][:]), f"vsb{half}")

        preload_weights(True)

        J0 = ([("p", 0, t) for t in range(12)]
              + [("o", 0, 0), ("p", 0, 12), ("o", 0, 1), ("p", 0, 13),
                 ("o", 0, 2), ("p", 0, 14), ("o", 0, 3), ("p", 0, 15)])
        run_jobs(J0)

        J1p = [("p", 1, t) for t in range(16)]
        issue_job(J1p[0])
        issue_job(J1p[1])

        for blk in range(2):
            pn = panel[blk % 2]
            for dsl in range(16):
                par = dsl % 2
                ev_panel = panel_done[blk % 2]
                if blk == 0:
                    pl_ev = ln_job(J1p[dsl])
                    if dsl + 2 < 16:
                        issue_job(J1p[dsl + 2])
                em.wait(nc.tensor, "t", w_tok[par])
                em.wait(nc.tensor, "t", ev_panel)
                em.wait(nc.tensor, "t", kv_ps_free[0])
                for h in range(16):
                    for j in range(2):
                        t_ = nc.tensor.matmul(kv_ps[0][:][:, j * 512:(j + 1) * 512],
                                              wsl2[par][:][:, h, :],
                                              pn[:, h, j * 512:j * 512 + 512],
                                              start=(h == 0), stop=(h == 15))
                ev_pe_h0 = em.inc(t_, "pe")
                em.wait(nc.tensor, "t", wq_tok[par])
                em.wait(nc.tensor, "t", q_ps_free)
                em.wait(nc.tensor, "t", xnq_done[blk])
                if blk == 0 and dsl == 0:
                    em.wait(nc.tensor, "t", ev_p0)
                    em.wait(nc.tensor, "t", ev_p0_ac)
                for h in range(16):
                    t_ = nc.tensor.matmul(q_ps[:][:, 0:512],
                                          wslq[par][:][:, h, :],
                                          xnq_fm[:, h, :],
                                          start=(h == 0), stop=(h == 15))
                ev_pe_q = em.inc(t_, "pe")
                if blk == 0:
                    tr_job(J1p[dsl], pl_ev)
                emit_kv_evac(blk, dsl, 0, ev_pe_h0)
                flush_norms(2)
                em.wait(nc.tensor, "t", kv_ps_free[1])
                for h in range(16):
                    for j in range(2):
                        c0 = 1024 + j * 512
                        t_ = nc.tensor.matmul(kv_ps[1][:][:, j * 512:(j + 1) * 512],
                                              wsl2[par][:][:, h, :],
                                              pn[:, h, c0:c0 + 512],
                                              start=(h == 0), stop=(h == 15))
                ev_pe_h1 = em.inc(t_, "pe")
                if dsl + 2 < 16:
                    em.wait(nc.gpsimd, "g", ev_pe_h1)
                    w_tok[par] = em.dma(nc.gpsimd.dma_start(
                        out=wsl2[par][:],
                        in_=wqkvT[:, 2048 + (dsl + 2) * 128:2048 + (dsl + 3) * 128]
                            .rearrange("(hc p) d -> p hc d", p=128)), f"wl{par}")
                    wq_tok[par] = em.dma(nc.gpsimd.dma_start(
                        out=wslq[par][:],
                        in_=wqkvT[:, (dsl + 2) * 128:(dsl + 3) * 128]
                            .rearrange("(hc p) d -> p hc d", p=128)), f"ql{par}")
                em.wait(nc.vector, "v", ev_pe_q)
                em.wait(nc.vector, "v", em.tok("p0"))
                em.wait(nc.vector, "v", q_dma)
                em.wait(nc.vector, "v", q_sq_done)
                ev_qst = em.inc(nc.vector.tensor_scalar(
                    qstage[:], q_ps[:], bq_cols[:, dsl:dsl + 1], None,
                    op0=ALU.add), "ve")
                q_ps_free = ev_qst
                em.wait(nc.sync, "y", ev_qst)
                q_dma = em.dma(nc.sync.dma_start(
                    out=q_b[dsl * 128:(dsl + 1) * 128, blk * 512:(blk + 1) * 512],
                    in_=qstage[:]), "qsb")
                while any(e[0] == 'q' and e[3] == par for e in norm_pending):
                    flush_norms(len(norm_pending) - 1)
                em.wait(nc.scalar, "s", ev_qst)
                em.wait(nc.scalar, "s", qsq_mm[par])
                ev_qsq = em.inc(nc.scalar.activation(qsq[par][:], qstage[:],
                                                     AF.Square), "ac")
                q_sq_done = ev_qsq
                norm_pending.append(('q', dsl, blk, par, ev_qsq))
                emit_kv_evac(blk, dsl, 1, ev_pe_h1)
                flush_norms(2)
            flush_norms(0)
            if blk == 0:
                preload_weights(False)
                run_jobs([("o", 1, t) for t in range(4)])

    ev_p2_pe = em.tok("pe")
    ev_p2_ve = em.tok("ve")
    ev_kv_done = [em.tok("ksb0"), em.tok("ksb1"), em.tok("vsb0"), em.tok("vsb1")]
    ev_q_done = em.tok("qsb")

    at_stack = ExitStack()
    at_sb = at_stack.enter_context(nc.sbuf_tensor("at_sb", [128, 16, 1024], BF16))
    p2kv_end = [ev_p2_pe, ev_p2_ve, em.tok("ac"), em.tok("po"), em.tok("ksb0"),
                em.tok("ksb1"), em.tok("vsb0"), em.tok("vsb1"), em.tok("qsb"),
                em.tok("xla0"), em.tok("xla1"), em.tok("xlb0"), em.tok("xlb1"),
                em.tok("wl0"), em.tok("wl1"), em.tok("ql0"), em.tok("ql1"),
                em.tok("p0")]
    phase_barrier(p2kv_end)
    d = nc.gpsimd.dma_start(out=at_sb[:],
                            in_=at_in.rearrange("(hc p) a -> p hc a", p=128))
    ev_at = em.dma(d, "wmisc")

    b_stack = ExitStack()
    bsb = lambda n, s_, d_: b_stack.enter_context(nc.sbuf_tensor(n, s_, d_))
    k_sb = [bsb(f"k_sb{i}", [128, 2, S], BF16) for i in range(2)]
    v_sb = [bsb(f"v_sb{i}", [128, 32, 256], BF16) for i in range(2)]
    q_sb = [bsb(f"q_sb{i}", [128, 2, OWN], BF16) for i in range(2)]
    NEXP = 4
    expb = [[bsb(f"expb{i}_{c}", [128, S], BF16) for c in range(2)] for i in range(NEXP)]
    NPAR = 3
    pcomb2 = [bsb(f"pcomb2_{i}", [128, S], BF16) for i in range(NPAR)]
    pt_sb = [bsb(f"pt_sb{i}", [128, 8, 128], BF16) for i in range(2)]
    sums = [bsb(f"sums{i}", [128, 16], F32) for i in range(3)]
    o_st = [bsb(f"o_st{i}", [128, 256], BF16) for i in range(2)]
    osq = [bsb("osq0", [128, 256], F32), bsb("osq1", [128, 256], F32)]
    osqq = [bsb("osqq0", [128, 256], F32), bsb("osqq1", [128, 256], F32)]
    brow = bsb("brow", [1, 128], F32)
    ps_s = [b_stack.enter_context(nc.psum_tensor(f"ps_s{i}", [128, 1024], F32))
            for i in range(2)]
    ps_t = [b_stack.enter_context(nc.psum_tensor(f"ps_t{i}", [128, 8, 128], BF16))
            for i in range(2)]
    ps_o = [b_stack.enter_context(nc.psum_tensor(f"ps_o{i}", [128, 256], F32))
            for i in range(2)]

    def emit_B_variant(c):
        P_MAP = [4 * c + r for r in range(4)] + [4 * (7 - c) + r for r in range(4)]

        def load_group(g_, par, after_pe):
            em.wait(nc.sync, "y", after_pe)
            em.wait(nc.sync, "y", ev_p2_pe)
            em.wait(nc.sync, "y", ev_p2_ve)
            for tk in ev_kv_done:
                em.wait(nc.sync, "y", tk)
            d1 = nc.sync.dma_start(out=k_sb[par][:],
                                   in_=k_b[g_ * 256:(g_ + 1) * 256, :]
                                   .rearrange("(cc p) t -> p cc t", p=128))
            em.dma(d1, f"bg{par}")
            d2 = nc.sync.dma_start(out=v_sb[par][:],
                                   in_=v_b[:, g_ * 256:(g_ + 1) * 256]
                                   .rearrange("(tt p) e -> p tt e", p=128))
            return em.dma(d2, f"bg{par}")

        def load_q(dh, par, after_pe):
            em.wait(nc.sync, "y", after_pe)
            em.wait(nc.sync, "y", ev_q_done)
            d1 = nc.sync.dma_start(out=q_sb[par][:],
                                   in_=q_b[dh * 256:(dh + 1) * 256, :]
                                   .rearrange("(cc p) t -> p cc t", p=128))
            return em.dma(d1, f"bq{par}")

        grp_tok = [None] * NG
        q_tok = [None] * NDH
        grp_tok[0] = load_group(0, 0, None)
        q_tok[0] = load_q(0, 0, None)
        grp_tok[1] = load_group(1, 1, None)
        q_tok[1] = load_q(1, 1, None)
        em.wait(nc.vector, "v", ev_p2_ve)
        for dh in range(NDH):
            g_ = dh // 2
            for cc in range(2):
                gc = g_ * 2 + cc
                for l in range(8):
                    p = P_MAP[l]
                    col = (dh * 2 + cc) * 8 + l
                    nc.vector.tensor_reduce(
                        brow[0:1, col:col + 1],
                        knmax[0:1, gc * 32:gc * 32 + p + 1],
                        axis=AX.X, op=ALU.max)
        nc.vector.drain()
        ev_v = em.inc(nc.vector.tensor_mul(brow[:], brow[:], qn2row[:]), "ve")
        em.wait(nc.scalar, "s", ev_v)
        nc.scalar.activation(brow[:], brow[:], AF.Ln, bias=eps_col[0:1, :])
        nc.scalar.drain()
        ev_a = em.inc(nc.scalar.activation(brow[:], brow[:], AF.Exp, scale=0.5), "ac")
        em.wait(nc.vector, "v", ev_a)
        ev_v = em.inc(nc.vector.tensor_scalar_mul(brow[:], brow[:], -float(SCALE)), "ve")
        em.wait(nc.tensor, "t", ev_v)
        em.wait(nc.tensor, "t", ev_p2_pe)
        t = nc.tensor.matmul(ps_o[0][:, 0:128], ones_row[:], brow[:],
                             start=True, stop=True)
        ev_pe = em.inc(t, "pe")
        em.wait(nc.vector, "v", ev_pe)
        ev_bound = em.inc(nc.vector.tensor_copy(bound_bc[:], ps_o[0][:, 0:128]), "ve")

        units = [(dh, l) for dh in range(NDH) for l in range(8)]
        NU = len(units)
        exp_done = {}
        comb_ev = {}
        pv_done = {}
        exp_ps_free = {}
        exp_free = {}
        pcomb_free = {}
        pt_free = [None, None]
        o_free = [ev_bound, None]
        o_dma = [None, None]
        o_st_done = [None, None]
        last_scores = {}

        def emit_scores(u):
            dh, l = units[u]
            par = u % NEXP
            spar = u % 3
            g_ = dh // 2
            gpar = g_ % 2
            qpar = dh % 2
            n = P_MAP[l] + 1
            nch = cdiv(n, 8)
            em.wait(nc.tensor, "t", grp_tok[g_])
            em.wait(nc.tensor, "t", q_tok[dh])
            ev_e = None
            for cc in range(2):
                for ch in range(nch):
                    t0 = ch * 8
                    sz = min(8, n - t0)
                    spp = (cc * nch + ch) % 2
                    em.wait(nc.tensor, "t", exp_ps_free.get(spp))
                    for j0 in range(0, sz, 4):
                        w = min(4, sz - j0) * 128
                        t = nc.tensor.matmul(
                            ps_s[spp][:, j0 * 128:j0 * 128 + w],
                            q_sb[qpar][:, cc, l * 128:(l + 1) * 128],
                            k_sb[gpar][:, cc, (t0 + j0) * 128:(t0 + j0) * 128 + w],
                            start=True, stop=True)
                    ev_pe = em.inc(t, "pe")
                    ev_ready = ev_pe
                    if ch == nch - 1:
                        em.wait(nc.vector, "v", ev_pe)
                        mm = nc.vector.tensor_tensor(
                            ps_s[spp][:, (sz - 1) * 128:sz * 128],
                            ps_s[spp][:, (sz - 1) * 128:sz * 128],
                            cmask[:], op=ALU.add)
                        ev_ready = em.inc(mm, "ve")
                    em.wait(nc.scalar, "s", ev_ready)
                    em.wait(nc.scalar, "s", exp_free.get(par))
                    em.wait(nc.scalar, "s", ev_bound)
                    col = (dh * 2 + cc) * 8 + l
                    ea = nc.scalar.activation(
                        expb[par][cc][:, t0 * 128:t0 * 128 + sz * 128],
                        ps_s[spp][:, 0:sz * 128],
                        AF.Exp, bias=bound_bc[:, col:col + 1], scale=float(SCALE),
                        accum_out=sums[spar][:, cc * 4 + ch:cc * 4 + ch + 1])
                    ev_e = em.inc(ea, "ac")
                    exp_ps_free[spp] = ev_e
            last_scores[dh] = em.tok("pe")
            return ev_e

        def emit_combine(u):
            dh, l = units[u]
            par = u % NEXP
            spar = u % 3
            n = P_MAP[l] + 1
            L = n * 128
            nch = cdiv(n, 8)
            em.wait(nc.vector, "v", exp_done[u])
            nc.vector.tensor_reduce(sums[spar][:, 8:9], sums[spar][:, 0:nch],
                                    axis=AX.X, op=ALU.add)
            nc.vector.tensor_reduce(sums[spar][:, 9:10], sums[spar][:, 4:4 + nch],
                                    axis=AX.X, op=ALU.add)
            nc.vector.drain()
            nc.vector.reciprocal(sums[spar][:, 10:11], sums[spar][:, 8:9])
            nc.vector.reciprocal(sums[spar][:, 11:12], sums[spar][:, 9:10])
            nc.vector.drain()
            nc.vector.tensor_mul(sums[spar][:, 12:13], sums[spar][:, 8:9],
                                 sums[spar][:, 11:12])
            nc.vector.drain()
            em.wait(nc.vector, "v", pcomb_free.get(u % NPAR))
            ev_a2 = em.inc(nc.vector.tensor_scalar(pcomb2[u % NPAR][:, 0:L],
                                                   expb[par][1][:, 0:L],
                                                   sums[spar][:, 12:13], lamneg[:],
                                                   op0=ALU.mult, op1=ALU.mult), "ve")
            em.wait(nc.gpsimd, "g", ev_a2)
            ins = nc.gpsimd.tensor_tensor(pcomb2[u % NPAR][:, 0:L],
                                          pcomb2[u % NPAR][:, 0:L],
                                          expb[par][0][:, 0:L], op=ALU.add)
            ev = em.inc(ins, "po")
            exp_free[par] = ev
            return ev

        def emit_pv(u):
            dh, l = units[u]
            par = u % NPAR
            g_ = dh // 2
            gpar = g_ % 2
            n = P_MAP[l] + 1
            opar = u % 2
            em.wait(nc.tensor, "t", comb_ev[u])
            em.wait(nc.tensor, "t", o_free[opar])
            first = True
            last_pe = None
            for t0 in range(0, n, 8):
                sz = min(8, n - t0)
                tp = (t0 // 8) % 2
                em.wait(nc.tensor, "t", pt_free[tp])
                for jj in range(sz):
                    tt = nc.tensor.transpose(
                        ps_t[tp][:, jj, :],
                        pcomb2[par][:, (t0 + jj) * 128:(t0 + jj + 1) * 128],
                        ident_bf[:])
                ev_pe = em.inc(tt, "pe")
                h1 = (sz + 1) // 2
                em.wait(nc.vector, "v", ev_pe)
                ev_v1 = em.inc(nc.vector.tensor_copy(pt_sb[tp][:, 0:h1, :],
                                                     ps_t[tp][:, 0:h1, :]), "ve")
                if sz > h1:
                    ev_v2 = em.inc(nc.vector.tensor_copy(pt_sb[tp][:, h1:sz, :],
                                                         ps_t[tp][:, h1:sz, :]), "ve")
                else:
                    ev_v2 = ev_v1
                em.wait(nc.tensor, "t", ev_v1)
                for jj in range(h1):
                    mm = nc.tensor.matmul(ps_o[opar][:, :], pt_sb[tp][:, jj, :],
                                          v_sb[gpar][:, t0 + jj, :],
                                          start=first, stop=(t0 + jj == n - 1))
                    first = False
                em.wait(nc.tensor, "t", ev_v2)
                for jj in range(h1, sz):
                    mm = nc.tensor.matmul(ps_o[opar][:, :], pt_sb[tp][:, jj, :],
                                          v_sb[gpar][:, t0 + jj, :],
                                          start=first, stop=(t0 + jj == n - 1))
                    first = False
                last_pe = em.inc(mm, "pe")
                pt_free[tp] = last_pe
            pcomb_free[u % NPAR] = last_pe
            return last_pe

        def emit_subln(u):
            dh, l = units[u]
            opar = u % 2
            spar = u % 3
            em.wait(nc.vector, "v", pv_done[u])
            em.wait(nc.vector, "v", o_st_done[opar])
            ev_c = em.inc(nc.vector.tensor_scalar(osq[opar][:], ps_o[opar][:, :],
                                                  sums[spar][:, 10:11], None,
                                                  op0=ALU.mult), "ve")
            em.wait(nc.scalar, "s", ev_c)
            nc.scalar.activation(osqq[opar][:], osq[opar][:], AF.Square,
                                 accum_out=sums[spar][:, 13:14])
            nc.scalar.drain()
            nc.scalar.activation(sums[spar][:, 14:15], sums[spar][:, 13:14], AF.Ln,
                                 bias=eps_col[:], scale=1.0 / 256)
            nc.scalar.drain()
            ev_a = em.inc(nc.scalar.activation(sums[spar][:, 15:16],
                                               sums[spar][:, 14:15], AF.Exp,
                                               scale=-0.5), "ac")
            em.wait(nc.vector, "v", ev_a)
            nc.vector.tensor_scalar(osq[opar][:], osq[opar][:], sums[spar][:, 15:16],
                                    None, op0=ALU.mult)
            nc.vector.drain()
            em.wait(nc.vector, "v", o_dma[opar])
            ev_v = em.inc(nc.vector.tensor_mul(o_st[opar][:], osq[opar][:], subln_sc[:]), "ve")
            o_st_done[opar] = ev_v
            em.wait(nc.sync, "y", ev_v)
            o_dma[opar] = em.dma(nc.sync.dma_start(
                out=o_b[l * 128:(l + 1) * 128, dh * 256:(dh + 1) * 256],
                in_=o_st[opar][:]), f"osb{opar}")
            return ev_c

        for u in range(NU):
            dh, l = units[u]
            exp_done[u] = emit_scores(u)
            if u >= 1:
                comb_ev[u - 1] = emit_combine(u - 1)
            if u >= 2:
                pv_done[u - 2] = emit_pv(u - 2)
            if u >= 3:
                o_free[(u - 3) % 2] = emit_subln(u - 3)
            if l == 1:
                g_ = dh // 2
                if dh % 2 == 0 and g_ + 1 < NG and grp_tok[g_ + 1] is None:
                    grp_tok[g_ + 1] = load_group(g_ + 1, (g_ + 1) % 2,
                                                 pv_done.get((2 * g_) * 8 - 1))
                if dh + 1 < NDH and q_tok[dh + 1] is None:
                    q_tok[dh + 1] = load_q(dh + 1, (dh + 1) % 2,
                                           last_scores.get(dh - 1))
        u = NU - 1
        comb_ev[u] = emit_combine(u)
        pv_done[u - 1] = emit_pv(u - 1)
        o_free[(u - 2) % 2] = emit_subln(u - 2)
        pv_done[u] = emit_pv(u)
        o_free[(u - 1) % 2] = emit_subln(u - 1)
        o_free[u % 2] = emit_subln(u)

    for _ch in ("bg0", "bg1", "bq0", "bq1", "osb0", "osb1"):
        em.ensure(_ch)
    pre_state = em.snap_state()
    ends = []
    for v in nc.Switch(engines=EV, index=[idxv] * len(EV), n=4):
        em.restore_state(pre_state)
        emit_B_variant(v)
        ends.append(dict(em.cnt))
    maxc = {q: max(e[q] for e in ends) for q in ends[0]}
    for qn in ("osb0", "osb1", "bg0", "bg1", "bq0", "bq1", "po"):
        assert all(e[qn] == ends[0][qn] for e in ends), qn
    eng_of = {"pe": nc.tensor, "ve": nc.vector, "ac": nc.scalar}
    for v in nc.Switch(engines=EV, index=[idxv] * len(EV), n=4):
        for qn in ("pe", "ve", "ac"):
            delta = maxc[qn] - ends[v][qn]
            if delta > 0:
                eng_of[qn].sem_inc(em.sems[qn], delta)
    em.restore_state(pre_state)
    em.cnt = dict(maxc)
    em.seen = {}
    b_stack.close()
    ev_B_pe = em.tok("pe")
    ev_B_ve = em.tok("ve")
    ev_o_done = [em.tok("osb0"), em.tok("osb1")]

    with ExitStack() as c12:
        cb = lambda n, s_, d_: c12.enter_context(nc.sbuf_tensor(n, s_, d_))
        xc = [cb("xc0", [128, HID], F32), cb("xc1", [128, HID], F32)]
        stat3 = cb("stat3", [128, 16], F32)
        xn_tm3 = cb("xn_tm3", [128, HID], BF16)
        xn_fm3 = cb("xn_fm3", [128, 16, 128], BF16)
        vsq = cb("vsq", [128, 1024], F32)
        arr2 = cb("arr2", [128, 64], F32)
        top8 = cb("top8", [128, 8], F32)
        zap = cb("zap", [128, 64], F32)
        routes = cb("routes", [128, 64], F32)
        vecs_bf = cb("vecs_bf", [128, 1024], BF16)
        vms = cb("vms", [128, 1024], F32)
        vmix = cb("vmix", [128, 16], F32)
        w_bf = cb("w_bf", [128, 1024], BF16)
        wfm_st = cb("wfm_st", [128, 8, 128], BF16)
        outw_sb = cb("outw_sb", [128, 16, HID], BF16)
        outbb = cb("outbb", [128, HID], F32)
        rowb = cb("rowb", [1, HID], F32)
        o_tm = [cb("o_tm0", [128, HID], BF16), cb("o_tm1", [128, HID], BF16)]
        o_fm = cb("o_fm", [128, 16, 128], BF16)
        s_scr = [cb("s_scr0", [128, HID], BF16), cb("s_scr1", [128, HID], BF16)]
        ps_vx = c12.enter_context(nc.psum_tensor("ps_vx", [128, 1024], F32))
        ps_s4 = c12.enter_context(nc.psum_tensor("ps_s4", [128, HID], F32))
        tr_ps3 = [c12.enter_context(nc.psum_tensor(f"tr_ps3{i}", [128, 4, 128], BF16)) for i in range(2)]

        phase_barrier([ev_B_pe, ev_B_ve, em.tok("ac"), em.tok("bg0"), em.tok("bg1"),
                       em.tok("bq0"), em.tok("bq1")] + ev_o_done)
        em.wait(nc.gpsimd, "g", em.tok("wmisc"))
        outw_src = outwT_in.rearrange("(hc p) d -> p hc d", p=128)
        tok_w_q = [em.dma(nc.gpsimd.dma_start(
            out=outw_sb[:, qq * 4:(qq + 1) * 4, :],
            in_=outw_src[:, qq * 4:(qq + 1) * 4, :]), ch)
            for qq, ch in enumerate(("wmisc", "wq1", "wq2", "wq3"))]
        em.wait(nc.sync, "y", ev_B_pe)
        em.wait(nc.sync, "y", ev_B_ve)
        tok_b = em.dma(nc.sync.dma_start(out=rowb[:], in_=outb_in[:]), "rw2")
        em.wait(nc.tensor, "t", tok_b)
        ev_ob = emit_row_bcast(rowb, ps_vx[:, 0:512], outbb, HID)

        trp = TrPool(tr_ps3)
        ln_last = [None, None]
        x_tok = {}
        o_tok = {}
        vx_free = [ev_ob]
        vecs_mm = None
        s4_mm = None
        ps4_free = None
        w_read = None
        wf_dma = None
        epi_ve = None
        tr_pe3 = None
        s_dma = [None, None]
        o_consumed = [None, None]

        def issue_x(t):
            par = t % 2
            em.wait(nc.sync, "y", ln_last[par])
            x_tok[t] = em.dma(nc.sync.dma_start(
                out=xc[par][:], in_=x_own[t * 128:(t + 1) * 128, :]), f"xl{par}")

        def issue_o(t):
            par = t % 2
            em.wait(nc.sync, "y", o_consumed[par])
            for tk in ev_o_done:
                em.wait(nc.sync, "y", tk)
            o_tok[t] = em.dma(nc.sync.dma_start(
                out=o_tm[par][:], in_=o_b[t * 128:(t + 1) * 128, :]), f"ol{par}")

        def emit_c1_epilogue(t):
            nonlocal w_read, wf_dma, epi_ve, vx_free
            em.wait(nc.scalar, "s", vecs_mm)
            em.wait(nc.scalar, "s", epi_ve)
            ev_sqv = em.inc(nc.scalar.activation(vsq[:], ps_vx[:], AF.Square), "ac")
            em.wait(nc.vector, "v", ev_sqv)
            nc.vector.tensor_reduce(arr2[:], vsq[:].rearrange("p (a r) -> p a r", r=16),
                                    axis=AX.X, op=ALU.add)
            nc.vector.drain()
            nc.vector.max(out=top8[:], in_=arr2[:])
            nc.vector.drain()
            nc.vector.memset(top8[:, 4:8], -1.0)
            nc.vector.drain()
            nc.vector.match_replace(out=zap[:], in_to_replace=top8[:],
                                    in_values=arr2[:], imm_value=-1.0)
            nc.vector.drain()
            nc.vector.tensor_scalar(routes[:], zap[:], -1.0, 0.25,
                                    op0=ALU.is_le, op1=ALU.mult)
            ev_vec = em.inc(nc.vector.tensor_copy(vecs_bf[:], ps_vx[:]), "ve")
            vx_free = [ev_sqv, ev_vec]
            nc.vector.drain()
            nc.vector.tensor_tensor(
                vms[:].rearrange("p (r a) -> p r a", a=64),
                vecs_bf[:].rearrange("p (a r) -> p r a", r=16),
                routes[:].rearrange("p a -> p () a").to_broadcast([128, 16, 64]),
                op=ALU.mult)
            nc.vector.drain()
            nc.vector.tensor_reduce(vmix[:], vms[:].rearrange("p (r a) -> p r a", a=64),
                                    axis=AX.X, op=ALU.add)
            nc.vector.drain()
            em.wait(nc.vector, "v", w_read)
            ev_w = em.inc(nc.vector.tensor_tensor(
                w_bf[:].rearrange("p (a r) -> p a r", r=16),
                routes[:].to_broadcast([128, 64, 16]),
                vmix[:].rearrange("p r -> p () r").to_broadcast([128, 64, 16]),
                op=ALU.mult), "ve")
            em.wait(nc.tensor, "t", ev_w)
            ev_t = trp.run([w_bf[:, ar * 128:(ar + 1) * 128] for ar in range(8)],
                           [wfm_st[:, g0:g0 + 4, :] for g0 in range(0, 8, 4)],
                           extra_v_wait=wf_dma)
            w_read = em.tok("pe")
            epi_ve = em.tok("ve")
            em.wait(nc.sync, "y", ev_t)
            wf_dma = em.dma(nc.sync.dma_start(
                out=wfm_b.rearrange("(c p) t -> p c t", p=128)[:, :, t * 128:(t + 1) * 128],
                in_=wfm_st[:]), "wf")

        def emit_c2_epilogue(t):
            nonlocal ps4_free
            par = t % 2
            em.wait(nc.vector, "v", s4_mm)
            em.wait(nc.vector, "v", ev_ob)
            em.wait(nc.vector, "v", s_dma[par])
            ev_s = em.inc(nc.vector.tensor_add(s_scr[par][:], ps_s4[:, :], outbb[:]), "ve")
            ps4_free = ev_s
            em.wait(nc.sync, "y", ev_s)
            s_dma[par] = em.dma(nc.sync.dma_start(
                out=s_b[t * 128:(t + 1) * 128, :], in_=s_scr[par][:]), f"ssb{par}")

        issue_x(0)
        issue_o(0)
        issue_o(1)
        for t in range(8):
            par = t % 2
            em.wait(nc.scalar, "s", x_tok[t])
            em.wait(nc.vector, "v", x_tok[t])
            ev_v = emit_ln(xc[par][:], tr_pe3, stat3[:, par * 8:par * 8 + 8], xn_tm3)
            ln_last[par] = ev_v
            if t + 1 < 8:
                issue_x(t + 1)
            if t >= 1:
                emit_c1_epilogue(t - 1)
                emit_c2_epilogue(t - 1)
            em.wait(nc.tensor, "t", o_tok[t])
            ev2 = trp.run([o_tm[par][:][:, hc * 128:(hc + 1) * 128] for hc in range(16)],
                          [o_fm[:, g0:g0 + 4, :] for g0 in range(0, 16, 4)],
                          extra_v_wait=s4_mm)
            em.wait(nc.tensor, "t", ev2)
            for tkq in tok_w_q:
                em.wait(nc.tensor, "t", tkq)
            em.wait(nc.tensor, "t", ps4_free)
            for hc in range(16):
                for j in range(4):
                    mm = nc.tensor.matmul(ps_s4[:, j * 512:(j + 1) * 512],
                                          o_fm[:, hc, :],
                                          outw_sb[:, hc, j * 512:(j + 1) * 512],
                                          start=(hc == 0), stop=(hc == 15))
            ev_pe2 = em.inc(mm, "pe")
            s4_mm = ev_pe2
            o_consumed[par] = ev_pe2
            if t + 2 < 8:
                issue_o(t + 2)
            em.wait(nc.tensor, "t", ev_v)
            ev = trp.run([xn_tm3[:, hc * 128:(hc + 1) * 128] for hc in range(16)],
                         [xn_fm3[:, g0:g0 + 4, :] for g0 in range(0, 16, 4)],
                         extra_v_wait=vecs_mm)
            tr_pe3 = em.tok("pe")
            em.wait(nc.tensor, "t", ev)
            em.wait(nc.tensor, "t", ev_at)
            for tk in vx_free:
                em.wait(nc.tensor, "t", tk)
            for h in range(16):
                for j in range(2):
                    t2 = nc.tensor.matmul(ps_vx[:][:, j * 512:(j + 1) * 512],
                                          xn_fm3[:, h, :],
                                          at_sb[:, h, j * 512:(j + 1) * 512],
                                          start=(h == 0), stop=(h == 15))
            ev_pe = em.inc(t2, "pe")
            vecs_mm = ev_pe
        emit_c1_epilogue(7)
        emit_c2_epilogue(7)
    at_stack.close()
    ev_c2_pe = em.tok("pe")
    ev_c2_ve = em.tok("ve")
    ev_c2_ac = em.tok("ac")

    with ExitStack() as c3:
        cb = lambda n, s_, d_: c3.enter_context(nc.sbuf_tensor(n, s_, d_))
        wt_sb = cb("wt_sb", [128, 16, HID], BF16)
        b2_sb = cb("b2_sb", [128, 8, HID], BF16)
        ffnbb = cb("ffnbb", [128, HID], F32)
        xc2 = [cb("xc20", [128, HID], F32), cb("xc21", [128, HID], F32)]
        lnscr4 = cb("lnscr4", [128, HID], F32)
        stat4 = cb("stat4", [128, 16], F32)
        xn_tm4 = cb("xn_tm4", [128, HID], BF16)
        xn_fm4 = cb("xn_fm4", [128, 16, 128], BF16)
        wfm_ld = [cb("wfm_ld0", [128, 8, 128], BF16), cb("wfm_ld1", [128, 8, 128], BF16)]
        s_tile = [cb("s_tile0", [128, HID], BF16), cb("s_tile1", [128, HID], BF16)]
        g_scr = cb("g_scr", [128, HID], F32)
        sg_scr = cb("sg_scr", [128, HID], F32)
        y_st = [cb("y_st0", [128, HID], F32), cb("y_st1", [128, HID], F32)]
        bd = [c3.enter_context(nc.psum_tensor(f"ps_bd{i}", [128, 1024], F32))
              for i in range(2)]
        tr_ps5 = [c3.enter_context(nc.psum_tensor(f"tr_ps5{i}", [128, 4, 128], BF16)) for i in range(2)]
        p0w3 = c3.enter_context(nc.psum_tensor("p0w3", [128, 512], F32))

        phase_barrier([ev_c2_pe, ev_c2_ve, ev_c2_ac, em.tok("ol0"), em.tok("ol1"),
                       em.tok("ssb0"), em.tok("ssb1"), em.tok("rw2"),
                       em.tok("wmisc"), em.tok("wq1"), em.tok("wq2"),
                       em.tok("wq3"), em.tok("xl0"), em.tok("xl1"),
                       em.tok("wf"), em.tok("po")])
        em.wait(nc.gpsimd, "g", em.tok("wmisc"))
        wt_src = wt_in.rearrange("(hc p) d -> p hc d", p=128)
        tok_wt_q = [em.dma(nc.gpsimd.dma_start(
            out=wt_sb[:, qq * 4:(qq + 1) * 4, :],
            in_=wt_src[:, qq * 4:(qq + 1) * 4, :]), ch)
            for qq, ch in enumerate(("wmisc", "wq1", "wq2", "wq3"))]
        b2_src = b2_in.rearrange("(ac p) d -> p ac d", p=128)
        tok_b2_q = [em.dma(nc.gpsimd.dma_start(
            out=b2_sb[:, qq * 4:(qq + 1) * 4, :],
            in_=b2_src[:, qq * 4:(qq + 1) * 4, :]), ch)
            for qq, ch in enumerate(("wb1", "wb2"))]
        em.wait(nc.sync, "y", ev_c2_pe)
        em.wait(nc.sync, "y", ev_c2_ve)
        em.wait(nc.sync, "y", ev_c2_ac)
        tok_fb = em.dma(nc.sync.dma_start(out=lnscr4[0:1, :], in_=fb_in[:]), "rw3")
        em.wait(nc.tensor, "t", tok_fb)
        ev_fbb = emit_row_bcast(lnscr4, p0w3, ffnbb, HID)
        fbb_pe = em.tok("pe")

        trp = TrPool(tr_ps5)
        ln_last = [None, None]
        s_tok = {}
        x_tok = {}
        wf_tok = {}
        y_dma = [None, None]
        eps_free = [None, None]
        bd_mm_s = {}
        tr_pe5 = None
        tr_t = {}

        def issue_tile(t):
            par = t % 2
            em.wait(nc.sync, "y", ln_last[par])
            em.wait(nc.sync, "y", bd_mm_s.get(2 * t - 3))
            if t == 0:
                em.wait(nc.sync, "y", em.tok("wf"))
            wf_tok[t] = em.dma(nc.sync.dma_start(
                out=wfm_ld[par][:],
                in_=wfm_b.rearrange("(c p) t -> p c t", p=128)[:, :, t * 128:(t + 1) * 128]),
                f"wfl{par}")
            x_tok[t] = em.dma(nc.sync.dma_start(out=xc2[par][:],
                                                in_=x_own[t * 128:(t + 1) * 128, :]),
                              f"xl{par}")
            for tk in (em.tok("ssb0"), em.tok("ssb1")) if t < 2 else ():
                em.wait(nc.sync, "y", tk)
            s_tok[t] = em.dma(nc.sync.dma_start(
                out=s_tile[par][:], in_=s_b[t * 128:(t + 1) * 128, :]), f"sl{par}")

        def emit_c3_epilogue(s):
            t, hj = divmod(s, 2)
            par = t % 2
            b = s % 2
            cols = slice(hj * 1024, (hj + 1) * 1024)
            nc.vector.drain()
            em.wait(nc.vector, "v", bd_mm_s[s])
            em.wait(nc.vector, "v", ev_fbb)
            ev_bd = em.inc(nc.vector.tensor_add(sg_scr[:, cols], bd[b][:, :],
                                                ffnbb[:, cols]), "ve")
            eps_free[b] = ev_bd
            em.wait(nc.scalar, "s", ev_bd)
            ev_g = em.inc(nc.scalar.activation(g_scr[:, cols], sg_scr[:, cols], AF.Exp,
                                               scale=-1.0), "ac")
            em.wait(nc.vector, "v", ev_g)
            nc.vector.tensor_scalar_add(g_scr[:, cols], g_scr[:, cols], 1.0)
            nc.vector.drain()
            nc.vector.reciprocal(g_scr[:, cols], g_scr[:, cols])
            nc.vector.drain()
            nc.vector.tensor_mul(g_scr[:, cols], g_scr[:, cols], sg_scr[:, cols])
            nc.vector.drain()
            em.wait(nc.vector, "v", s_tok[t])
            nc.vector.tensor_mul(sg_scr[:, cols], s_tile[par][:, cols], g_scr[:, cols])
            nc.vector.drain()
            em.wait(nc.vector, "v", y_dma[par])
            ev_y = em.inc(nc.vector.tensor_add(y_st[par][:, cols], sg_scr[:, cols],
                                               xc2[par][:, cols]), "ve")
            if hj == 1:
                ln_last[par] = ev_y
            em.wait(nc.sync, "y", ev_y)
            y_dma[par] = em.dma(nc.sync.dma_start(
                out=y_out[t * 128:(t + 1) * 128, cols], in_=y_st[par][:, cols]),
                f"yb{par}")

        issue_tile(0)
        issue_tile(1)
        for s in range(16):
            t, hj = divmod(s, 2)
            par = t % 2
            b = s % 2
            if hj == 0:
                em.wait(nc.scalar, "s", x_tok[t])
                em.wait(nc.scalar, "s", fbb_pe)
                em.wait(nc.vector, "v", x_tok[t])
                ev_v = emit_ln(xc2[par][:], tr_pe5, stat4[:, par * 8:par * 8 + 8],
                               xn_tm4)
                em.wait(nc.tensor, "t", ev_v)
                ev = trp.run([xn_tm4[:, hc * 128:(hc + 1) * 128] for hc in range(16)],
                             [xn_fm4[:, g0:g0 + 4, :] for g0 in range(0, 16, 4)],
                             extra_v_wait=bd_mm_s.get(2 * t - 1))
                tr_pe5 = em.tok("pe")
                tr_t[t] = ev
            em.wait(nc.tensor, "t", tr_t[t])
            for tkq in tok_wt_q:
                em.wait(nc.tensor, "t", tkq)
            for tkq in tok_b2_q:
                em.wait(nc.tensor, "t", tkq)
            em.wait(nc.tensor, "t", eps_free[b])
            for hc in range(16):
                for j in range(2):
                    mm = nc.tensor.matmul(bd[b][:, j * 512:(j + 1) * 512],
                                          xn_fm4[:, hc, :],
                                          wt_sb[:, hc, hj * 1024 + j * 512:hj * 1024 + (j + 1) * 512],
                                          start=(hc == 0), stop=False)
            em.wait(nc.tensor, "t", wf_tok[t])
            for ar in range(8):
                for j in range(2):
                    mm = nc.tensor.matmul(bd[b][:, j * 512:(j + 1) * 512],
                                          wfm_ld[par][:][:, ar, :],
                                          b2_sb[:, ar, hj * 1024 + j * 512:hj * 1024 + (j + 1) * 512],
                                          start=False, stop=(ar == 7))
            bd_mm_s[s] = em.inc(mm, "pe")
            if s >= 1:
                emit_c3_epilogue(s - 1)
                if s % 2 == 0 and s // 2 + 1 <= 7:
                    issue_tile(s // 2 + 1)
        emit_c3_epilogue(15)
        em.wait(nc.sync, "y", em.tok("yb0"))
        em.wait(nc.sync, "y", em.tok("yb1"))

    stack.close()
    return nc


_NC_CACHE = [None]


def _prep(inputs):
    hid = np.asarray(inputs["L0_attn_hidden_states"], dtype=np.float32)
    wqkvT = np.ascontiguousarray(np.asarray(inputs["L0_attn_Wqkv_w"], np.float32).T)
    bq = np.ascontiguousarray(np.asarray(inputs["L0_attn_Wqkv_b"], np.float32).reshape(1, 4096))
    lrows = np.ascontiguousarray(np.stack([
        np.asarray(inputs["L0_attn_lambda_q1"], np.float32),
        np.asarray(inputs["L0_attn_lambda_k1"], np.float32),
        np.asarray(inputs["L0_attn_lambda_q2"], np.float32),
        np.asarray(inputs["L0_attn_lambda_k2"], np.float32)]))
    subln = np.ascontiguousarray(np.asarray(inputs["L0_attn_subln_w"], np.float32).reshape(1, 256))
    outwT = np.ascontiguousarray(np.asarray(inputs["L0_attn_out_w"], np.float32).T)
    outb = np.ascontiguousarray(np.asarray(inputs["L0_attn_out_b"], np.float32).reshape(1, 2048))
    A = np.asarray(inputs["L0_ffn_A"], np.float32)
    Bm = np.asarray(inputs["L0_ffn_B"], np.float32)
    Wf = np.asarray(inputs["L0_ffn_W"], np.float32)
    fb = np.ascontiguousarray(np.asarray(inputs["L0_ffn_bias"], np.float32).reshape(1, 2048))
    a_t = np.ascontiguousarray(A.reshape(1024, 2048).T)
    b2 = np.ascontiguousarray(Bm.transpose(0, 2, 1).reshape(1024, 2048))
    w_t = np.ascontiguousarray(Wf.T)

    in_maps = []
    rows_per_core = []
    for i in range(8):
        b = i // 4
        c = i % 4
        rows = np.r_[512 * c:512 * (c + 1), 512 * (7 - c):512 * (8 - c)]
        rows_per_core.append((b, rows))
        in_maps.append({
            "x_full": np.ascontiguousarray(hid[b]),
            "x_own": np.ascontiguousarray(hid[b][rows]),
            "wqkvT": wqkvT, "bqkv": bq, "lrows": lrows, "subln": subln,
            "outwT": outwT, "outb": outb, "a_t": a_t, "b2": b2,
            "w_t": w_t, "fbias": fb,
        })
    return in_maps, rows_per_core


def kernel(**inputs):
    in_maps, rows_per_core = _prep(inputs)
    if _NC_CACHE[0] is None:
        _NC_CACHE[0] = build_program()
    nc = _NC_CACHE[0]
    res = run_bass_kernel_spmd(nc, in_maps, core_ids=list(range(8)))
    out = np.zeros((2, 4096, 2048), np.float32)
    for i in range(8):
        b, rows = rows_per_core[i]
        out[b][rows] = np.asarray(res.results[i]["y"])
    return out


if __name__ == "__main__":
    import time
    t0 = time.time()
    nc = build_program()
    print("built instructions:", nc.n_instructions(), "in", round(time.time() - t0, 1), "s")


# version 27 (final)
# speedup vs baseline: 1.2311x; 1.0111x over previous
import math
import numpy as np
from contextlib import ExitStack

import concourse.bass as bass
import concourse.mybir as mybir
from concourse.bass_utils import run_bass_kernel_spmd

F32 = mybir.dt.float32
BF16 = mybir.dt.bfloat16
AF = mybir.ActivationFunctionType
ALU = mybir.AluOpType
AX = mybir.AxisListType

S = 4096
HID = 2048
OWN = 1024
NDH = 8
NG = 4
SCALE = 128.0 ** -0.5
LI = 0.8 - 0.6 * math.exp(-0.3)
EPS = 1e-5
MASKVAL = -30000.0


def cdiv(a, b):
    return (a + b - 1) // b


class Em:
    def __init__(self, nc, stack):
        self.nc = nc
        self.stack = stack
        self.sems = {}
        self.cnt = {}
        self.step = {}
        for name, step in [("pe", 1), ("ve", 1), ("ac", 1), ("po", 1)]:
            self.sems[name] = stack.enter_context(nc.semaphore(name))
            self.cnt[name] = 0
            self.step[name] = step
        self.seen = {}

    def ensure(self, ch):
        if ch not in self.sems:
            self.sems[ch] = self.stack.enter_context(self.nc.semaphore(ch))
            self.cnt[ch] = 0
            self.step[ch] = 16

    def dma(self, instr, ch):
        if ch not in self.sems:
            self.sems[ch] = self.stack.enter_context(self.nc.semaphore(ch))
            self.cnt[ch] = 0
            self.step[ch] = 16
        return self.inc(instr, ch)

    def inc(self, instr, q, k=1):
        v = self.cnt[q] + self.step[q] * k
        self.cnt[q] = v
        instr.then_inc(self.sems[q], self.step[q] * k)
        return (q, v)

    def tok(self, q):
        return (q, self.cnt[q])

    def wait(self, eng, ename, tok):
        if tok is None:
            return
        q, v = tok
        if v <= 0:
            return
        key = (ename, q)
        if self.seen.get(key, 0) >= v:
            return
        eng.wait_ge(self.sems[q], v)
        self.seen[key] = v

    def snap_state(self):
        return (dict(self.cnt), dict(self.seen))

    def restore_state(self, st):
        self.cnt = dict(st[0])
        self.seen = dict(st[1])


def build_program():
    nc = bass.Bass("TRN2", target_bir_lowering=False, debug=False)

    x_full = nc.dram_tensor("x_full", [S, HID], F32, kind="ExternalInput").ap()
    x_own = nc.dram_tensor("x_own", [OWN, HID], F32, kind="ExternalInput").ap()
    wqkvT = nc.dram_tensor("wqkvT", [HID, 4096], F32, kind="ExternalInput").ap()
    bqkv = nc.dram_tensor("bqkv", [1, 4096], F32, kind="ExternalInput").ap()
    lrow_in = nc.dram_tensor("lrows", [4, 128], F32, kind="ExternalInput").ap()
    subln_in = nc.dram_tensor("subln", [1, 256], F32, kind="ExternalInput").ap()
    outwT_in = nc.dram_tensor("outwT", [HID, HID], F32, kind="ExternalInput").ap()
    outb_in = nc.dram_tensor("outb", [1, HID], F32, kind="ExternalInput").ap()
    at_in = nc.dram_tensor("a_t", [HID, 1024], F32, kind="ExternalInput").ap()
    b2_in = nc.dram_tensor("b2", [1024, HID], F32, kind="ExternalInput").ap()
    wt_in = nc.dram_tensor("w_t", [HID, HID], F32, kind="ExternalInput").ap()
    fb_in = nc.dram_tensor("fbias", [1, HID], F32, kind="ExternalInput").ap()
    y_out = nc.dram_tensor("y", [OWN, HID], F32, kind="ExternalOutput").ap()

    k_b = nc.dram_tensor("k_b", [1024, S], BF16).ap()
    v_b = nc.dram_tensor("v_b", [S, 1024], BF16).ap()
    q_b = nc.dram_tensor("q_b", [2048, OWN], BF16).ap()
    o_b = nc.dram_tensor("o_b", [OWN, 2048], BF16).ap()
    s_b = nc.dram_tensor("s_b", [OWN, HID], BF16).ap()
    wfm_b = nc.dram_tensor("wfm_b", [1024, OWN], BF16).ap()

    stack = ExitStack()
    sb = lambda n, s_, d: stack.enter_context(nc.sbuf_tensor(n, s_, d))
    em = Em(nc, stack)
    EV = [nc.tensor, nc.vector, nc.scalar, nc.sync, nc.gpsimd]

    ident_bf = sb("ident_bf", [128, 128], BF16)
    cmask = sb("cmask", [128, 128], BF16)
    ones_row = sb("ones_row", [1, 128], F32)
    ones_col_bf = sb("ones_col_bf", [128, 1], BF16)
    bq_cols = sb("bq_cols", [128, 32], F32)
    lamneg = sb("lamneg", [128, 1], F32)
    subln_sc = sb("subln_sc", [128, 256], F32)
    qn2row = sb("qn2row", [1, 128], F32)
    knmax = sb("knmax", [1, 256], F32)
    bound_bc = sb("bound_bc", [128, 128], F32)
    eps_col = sb("eps_col", [128, 1], F32)

    def emit_ln(x_tile, scr, stat, out_bf):
        em.wait(nc.scalar, "s", scr)
        nc.scalar.activation(out_bf[:], x_tile, AF.Copy, accum_out=stat[:, 0:1])
        nc.scalar.drain()
        ev_a = em.inc(nc.scalar.activation(out_bf[:], x_tile, AF.Square,
                                           accum_out=stat[:, 1:2]), "ac")
        em.wait(nc.vector, "v", ev_a)
        nc.vector.tensor_scalar_mul(stat[:, 2:3], stat[:, 0:1], 1.0 / HID)
        nc.vector.tensor_scalar_mul(stat[:, 3:4], stat[:, 1:2], 1.0 / HID)
        nc.vector.drain()
        nc.vector.tensor_mul(stat[:, 4:5], stat[:, 2:3], stat[:, 2:3])
        nc.vector.drain()
        ev_v = em.inc(nc.vector.tensor_sub(stat[:, 5:6], stat[:, 3:4], stat[:, 4:5]), "ve")
        em.wait(nc.scalar, "s", ev_v)
        nc.scalar.activation(stat[:, 6:7], stat[:, 5:6], AF.Ln, bias=eps_col[:])
        nc.scalar.drain()
        ev_a = em.inc(nc.scalar.activation(stat[:, 7:8], stat[:, 6:7], AF.Exp,
                                           scale=-0.5), "ac")
        em.wait(nc.vector, "v", ev_a)
        nc.vector.tensor_mul(stat[:, 4:5], stat[:, 2:3], stat[:, 7:8])
        nc.vector.drain()
        nc.vector.tensor_scalar_mul(stat[:, 4:5], stat[:, 4:5], -1.0)
        nc.vector.drain()
        return em.inc(nc.vector.tensor_scalar(out_bf[:], x_tile, stat[:, 7:8],
                                              stat[:, 4:5], op0=ALU.mult,
                                              op1=ALU.add), "ve")

    class TrPool:

        def __init__(self, ps2):
            self.ps = ps2
            self.free = [None, None]
            self.i = 0

        def run(self, srcs, dsts, extra_v_wait=None):
            ev = None
            gi = 0
            for g0 in range(0, len(srcs), 4):
                grp = srcs[g0:g0 + 4]
                bank = self.i % 2
                self.i += 1
                em.wait(nc.tensor, "t", self.free[bank])
                for j, s_ in enumerate(grp):
                    tt = nc.tensor.transpose(self.ps[bank][:][:, j, :], s_, ident_bf[:])
                ev_pe = em.inc(tt, "pe")
                em.wait(nc.vector, "v", ev_pe)
                if extra_v_wait is not None:
                    em.wait(nc.vector, "v", extra_v_wait)
                ev = em.inc(nc.vector.tensor_copy(dsts[gi],
                                                  self.ps[bank][:][:, 0:len(grp), :]), "ve")
                self.free[bank] = ev
                gi += 1
            return ev

    def emit_row_bcast(row_ap, psum, dst, ncols, scale=None):
        ev = None
        for j0 in range(0, ncols, 512):
            w = min(512, ncols - j0)
            em.wait(nc.tensor, "t", em.tok("ve"))
            t = nc.tensor.matmul(psum[:, 0:w], ones_row[:], row_ap[0:1, j0:j0 + w],
                                 start=True, stop=True)
            ev_pe = em.inc(t, "pe")
            em.wait(nc.vector, "v", ev_pe)
            if scale is None:
                ins = nc.vector.tensor_copy(dst[:, j0:j0 + w], psum[:, 0:w])
            else:
                ins = nc.vector.tensor_scalar_mul(dst[:, j0:j0 + w], psum[:, 0:w], scale)
            ev = em.inc(ins, "ve")
        return ev

    def phase_barrier(tokens):
        for eng, en in [(nc.sync, "y"), (nc.gpsimd, "g"), (nc.vector, "v"),
                        (nc.scalar, "s"), (nc.tensor, "t")]:
            for tk in tokens:
                em.wait(eng, en, tk)

    g = nc.gpsimd
    g.memset(ident_bf[:], 0.0)
    g.memset(cmask[:], 0.0)
    g.memset(ones_row[:], 1.0)
    g.memset(ones_col_bf[:], 1.0)
    g.memset(eps_col[:], float(EPS))
    g.drain()
    g.affine_select(out=ident_bf[:], in_=ident_bf[:], compare_op=ALU.not_equal,
                    fill=1.0, base=0, pattern=[[-1, 128]], channel_multiplier=1)
    tm = g.affine_select(out=cmask[:], in_=cmask[:], compare_op=ALU.is_ge,
                         fill=MASKVAL, base=0, pattern=[[-1, 128]],
                         channel_multiplier=1)
    ev_const = em.inc(tm, "po")

    pid = nc.partition_id(engines=[e.engine for e in EV])
    idxv = nc.snap(pid % 4, engines=[e.engine for e in EV])

    with ExitStack() as p2kv:
        lcols = p2kv.enter_context(nc.sbuf_tensor("lcols", [128, 4], F32))
        dots = p2kv.enter_context(nc.sbuf_tensor("dots", [1, 4], F32))
        lam_t = p2kv.enter_context(nc.sbuf_tensor("lam_t", [1, 4], F32))
        xk0 = p2kv.enter_context(nc.sbuf_tensor("xk0", [128, HID], F32))
        xk1 = p2kv.enter_context(nc.sbuf_tensor("xk1", [128, HID], F32))
        stat2 = p2kv.enter_context(nc.sbuf_tensor("stat2", [128, 16], F32))
        xn_tm2 = p2kv.enter_context(nc.sbuf_tensor("xn_tm2", [128, HID], BF16))
        xn_tm2_b = p2kv.enter_context(nc.sbuf_tensor("xn_tm2_b", [128, HID], BF16))
        panelA = p2kv.enter_context(nc.sbuf_tensor("panelA", [128, 16, 2048], BF16))
        panelB = p2kv.enter_context(nc.sbuf_tensor("panelB", [128, 16, 2048], BF16))
        xnq_fm = p2kv.enter_context(nc.sbuf_tensor("xnq_fm", [128, 16, 512], BF16))
        wsl20 = p2kv.enter_context(nc.sbuf_tensor("wsl20", [128, 16, 128], BF16))
        wsl21 = p2kv.enter_context(nc.sbuf_tensor("wsl21", [128, 16, 128], BF16))
        wslq0 = p2kv.enter_context(nc.sbuf_tensor("wslq0", [128, 16, 128], BF16))
        wslq1 = p2kv.enter_context(nc.sbuf_tensor("wslq1", [128, 16, 128], BF16))
        kvstage0 = p2kv.enter_context(nc.sbuf_tensor("kvstage0", [128, 1024], BF16))
        kvstage1 = p2kv.enter_context(nc.sbuf_tensor("kvstage1", [128, 1024], BF16))
        ksqA = p2kv.enter_context(nc.sbuf_tensor("ksqA", [128, 1024], BF16))
        ksqB = p2kv.enter_context(nc.sbuf_tensor("ksqB", [128, 1024], BF16))
        vt0 = p2kv.enter_context(nc.sbuf_tensor("vt0", [128, 8, 128], BF16))
        vt1 = p2kv.enter_context(nc.sbuf_tensor("vt1", [128, 8, 128], BF16))
        qstage = p2kv.enter_context(nc.sbuf_tensor("qstage", [128, 512], BF16))
        qsqA = p2kv.enter_context(nc.sbuf_tensor("qsqA", [128, 512], BF16))
        qsqB = p2kv.enter_context(nc.sbuf_tensor("qsqB", [128, 512], BF16))
        tr_ps20 = p2kv.enter_context(nc.psum_tensor("tr_ps20", [128, 4, 128], BF16))
        tr_ps21 = p2kv.enter_context(nc.psum_tensor("tr_ps21", [128, 4, 128], BF16))
        kv_ps0 = p2kv.enter_context(nc.psum_tensor("kv_ps0", [128, 1024], F32))
        kv_ps1 = p2kv.enter_context(nc.psum_tensor("kv_ps1", [128, 1024], F32))
        n_ps20 = p2kv.enter_context(nc.psum_tensor("n_ps20", [1, 512], F32))
        p0_sub = ExitStack()
        p0_ps = p0_sub.enter_context(nc.psum_tensor("p0_ps", [128, 256], F32))
        xk = [xk0, xk1]; wsl2 = [wsl20, wsl21]; wslq = [wslq0, wslq1]
        kvstage = [kvstage0, kvstage1]; ksq = [ksqA, ksqB]; qsq = [qsqA, qsqB]
        vt = [vt0, vt1]
        tr_ps2 = [tr_ps20, tr_ps21]; kv_ps = [kv_ps0, kv_ps1]
        panel = [panelA, panelB]
        xn_tm2b = [xn_tm2, xn_tm2_b]

        em.wait(nc.tensor, "t", ev_const)
        em.wait(nc.scalar, "s", ev_const)
        em.wait(nc.vector, "v", ev_const)
        trp = TrPool(tr_ps2)

        ln_last = [None, None]
        tr_pe2 = [None, None]
        panel_done = [None, None]
        xnq_done = [None, None]
        job_tok = {}
        job_par = {}
        jseq = [0]

        def issue_job(job):
            par = jseq[0] % 2
            jseq[0] += 1
            job_par[job] = par
            em.wait(nc.sync, "y", ln_last[par])
            kind, blk_, t = job
            if kind == "p":
                src, r0 = x_full, (blk_ * 16 + t) * 128
            else:
                src, r0 = x_own, blk_ * 512 + t * 128
            tka = em.dma(nc.sync.dma_start(
                out=xk[par][:, 0:1024], in_=src[r0:r0 + 128, 0:1024]), f"xla{par}")
            tkb = em.dma(nc.sync.dma_start(
                out=xk[par][:, 1024:2048], in_=src[r0:r0 + 128, 1024:2048]), f"xlb{par}")
            job_tok[job] = (tka, tkb)

        def ln_job(job):
            par = job_par[job]
            for tk in job_tok[job]:
                em.wait(nc.scalar, "s", tk)
                em.wait(nc.vector, "v", tk)
            ev_v = emit_ln(xk[par][:], tr_pe2[par], stat2[:, par * 8:par * 8 + 8],
                           xn_tm2b[par])
            ln_last[par] = ev_v
            return ev_v

        def tr_job(job, ev_v):
            par = job_par[job]
            kind, blk_, t = job
            em.wait(nc.tensor, "t", ev_v)
            if kind == "p":
                dsts = [panel[blk_ % 2][:, g0:g0 + 4, t * 128:(t + 1) * 128]
                        for g0 in range(0, 16, 4)]
            else:
                dsts = [xnq_fm[:, g0:g0 + 4, t * 128:(t + 1) * 128]
                        for g0 in range(0, 16, 4)]
            ev = trp.run([xn_tm2b[par][:, hc * 128:(hc + 1) * 128] for hc in range(16)],
                         dsts)
            tr_pe2[par] = em.tok("pe")
            if kind == "p":
                panel_done[blk_ % 2] = ev
            else:
                xnq_done[blk_] = ev
            return ev

        def run_jobs(jobs):
            for j in jobs[:2]:
                issue_job(j)
            prev = None
            for i, j in enumerate(jobs):
                ev_v = ln_job(j)
                if i + 2 < len(jobs):
                    issue_job(jobs[i + 2])
                if prev is not None:
                    tr_job(*prev)
                prev = (j, ev_v)
            tr_job(*prev)

        em.dma(nc.sync.dma_start(
            out=bq_cols[:], in_=bqkv.rearrange("o (j p) -> p (o j)", p=128),
            allow_slow_non_contiguous=True), "p0")
        em.dma(nc.sync.dma_start(
            out=lcols[:], in_=lrow_in.rearrange("r c -> c r"),
            allow_slow_non_contiguous=True), "p0")
        em.dma(nc.sync.dma_start(out=subln_sc[0:1, :], in_=subln_in[:]), "p0")
        tok_p0 = em.tok("p0")
        em.wait(nc.tensor, "t", tok_p0)
        nc.tensor.matmul(p0_ps[0:1, 250:251], lcols[:, 0:1], lcols[:, 1:2],
                         start=True, stop=True)
        t_ = nc.tensor.matmul(p0_ps[0:1, 251:252], lcols[:, 2:3], lcols[:, 3:4],
                              start=True, stop=True)
        ev_pe = em.inc(t_, "pe")
        em.wait(nc.scalar, "s", ev_pe)
        em.inc(nc.scalar.activation(dots[:, 0:2], p0_ps[0:1, 250:252], AF.Exp), "ac")
        ev_p0_ac = em.tok("ac")
        em.wait(nc.vector, "v", ev_p0_ac)
        nc.vector.tensor_sub(lam_t[:, 0:1], dots[:, 0:1], dots[:, 1:2])
        nc.vector.drain()
        ev_lam = em.inc(nc.vector.tensor_scalar(lam_t[:, 1:2], lam_t[:, 0:1],
                                                float(LI), -1.0,
                                                op0=ALU.add, op1=ALU.mult), "ve")
        em.wait(nc.tensor, "t", ev_lam)
        t_ = nc.tensor.matmul(p0_ps[:, 36:37], ones_row[:], lam_t[:, 1:2],
                              start=True, stop=True)
        ev_pe = em.inc(t_, "pe")
        em.wait(nc.vector, "v", ev_pe)
        ev_lnb = em.inc(nc.vector.tensor_copy(lamneg[:], p0_ps[:, 36:37]), "ve")
        em.wait(nc.tensor, "t", ev_lnb)
        t_ = nc.tensor.matmul(p0_ps[:, 0:256], ones_row[:], subln_sc[0:1, :],
                              start=True, stop=True)
        ev_pe = em.inc(t_, "pe")
        em.wait(nc.vector, "v", ev_pe)
        em.inc(nc.vector.tensor_scalar_mul(subln_sc[:], p0_ps[:, 0:256],
                                           float(1.0 - LI)), "ve")
        ev_p0 = em.tok("ve")
        p0_sub.close()
        q_ps = p2kv.enter_context(nc.psum_tensor("q_ps", [128, 512], F32))

        w_tok = [None, None]
        wq_tok = [None, None]
        kv_ps_free = [None, None]
        q_ps_free = None
        k_dma = [None, None]
        k_sq_done = [None, None]
        ksq_mm = [None, None]
        q_dma = None
        q_sq_done = None
        qsq_mm = [None, None]
        nrm_free = None
        vt_dma = [None, None]
        v_tr_done = [None, None]
        norm_pending = []

        def flush_norms(keep):
            nonlocal nrm_free
            while len(norm_pending) > keep:
                ent = norm_pending.pop(0)
                if ent[0] == 'k':
                    _, dsl0, half0, blk0, ev_sq0 = ent
                    for j in range(2):
                        em.wait(nc.tensor, "t", ev_sq0)
                        em.wait(nc.tensor, "t", nrm_free)
                        t0 = nc.tensor.matmul(n_ps20[0:1, :], ones_col_bf[:],
                                              ksq[half0][:, j * 512:(j + 1) * 512],
                                              start=True, stop=True)
                        ev_pe0 = em.inc(t0, "pe")
                        em.wait(nc.vector, "v", ev_pe0)
                        blkpos = blk0 * 16 + half0 * 8 + j * 4
                        nrm_free = em.inc(nc.vector.tensor_reduce(
                            knmax[0:1, dsl0 * 32 + blkpos:dsl0 * 32 + blkpos + 4],
                            n_ps20[0:1, :].rearrange("o (a b) -> o a b", b=128),
                            axis=AX.X, op=ALU.max), "ve")
                    ksq_mm[half0] = em.tok("pe")
                else:
                    _, dsl0, blk0, par0, ev_sq0 = ent
                    em.wait(nc.tensor, "t", ev_sq0)
                    em.wait(nc.tensor, "t", nrm_free)
                    t0 = nc.tensor.matmul(n_ps20[0:1, :], ones_col_bf[:],
                                          qsq[par0][:, :], start=True, stop=True)
                    ev_pe0 = em.inc(t0, "pe")
                    em.wait(nc.vector, "v", ev_pe0)
                    nrm_free = em.inc(nc.vector.tensor_reduce(
                        qn2row[0:1, dsl0 * 8 + blk0 * 4:dsl0 * 8 + blk0 * 4 + 4],
                        n_ps20[0:1, :].rearrange("o (a b) -> o a b", b=128),
                        axis=AX.X, op=ALU.max), "ve")
                    qsq_mm[par0] = em.tok("pe")

        def preload_weights(first):
            if not first:
                em.wait(nc.gpsimd, "g", em.tok("pe"))
            for d2 in range(2):
                w_tok[d2] = em.dma(nc.gpsimd.dma_start(
                    out=wsl2[d2][:],
                    in_=wqkvT[:, 2048 + d2 * 128:2048 + (d2 + 1) * 128]
                        .rearrange("(hc p) d -> p hc d", p=128)), f"wl{d2}")
                wq_tok[d2] = em.dma(nc.gpsimd.dma_start(
                    out=wslq[d2][:],
                    in_=wqkvT[:, d2 * 128:(d2 + 1) * 128]
                        .rearrange("(hc p) d -> p hc d", p=128)), f"ql{d2}")

        def emit_kv_evac(blk, dsl, half, ev_pe_mm):
            em.wait(nc.vector, "v", ev_pe_mm)
            em.wait(nc.vector, "v", em.tok("p0"))
            em.wait(nc.vector, "v", k_dma[half])
            em.wait(nc.vector, "v", k_sq_done[half])
            em.wait(nc.vector, "v", v_tr_done[half])
            ev_st = em.inc(nc.vector.tensor_scalar(
                kvstage[half][:], kv_ps[half][:], bq_cols[:, 16 + dsl:17 + dsl],
                None, op0=ALU.add), "ve")
            kv_ps_free[half] = ev_st
            if dsl < 8:
                em.wait(nc.sync, "y", ev_st)
                k_dma[half] = em.dma(nc.sync.dma_start(
                    out=k_b[dsl * 128:(dsl + 1) * 128,
                            blk * 2048 + half * 1024:blk * 2048 + half * 1024 + 1024],
                    in_=kvstage[half][:]), f"ksb{half}")
                while any(e[0] == 'k' and e[2] == half for e in norm_pending):
                    flush_norms(len(norm_pending) - 1)
                em.wait(nc.scalar, "s", ev_st)
                em.wait(nc.scalar, "s", ksq_mm[half])
                ev_sq = em.inc(nc.scalar.activation(ksq[half][:], kvstage[half][:],
                                                    AF.Square), "ac")
                k_sq_done[half] = ev_sq
                norm_pending.append(('k', dsl, half, blk, ev_sq))
            else:
                em.wait(nc.tensor, "t", ev_st)
                ev_tr = trp.run([kvstage[half][:, tt * 128:(tt + 1) * 128]
                                 for tt in range(8)],
                                [vt[half][:, g0:g0 + 4, :] for g0 in range(0, 8, 4)],
                                extra_v_wait=vt_dma[half])
                v_tr_done[half] = em.tok("pe")
                em.wait(nc.sync, "y", ev_tr)
                vt_dma[half] = em.dma(nc.sync.dma_start(
                    out=v_b.rearrange("(a p) e -> p a e", p=128)
                        [:, blk * 16 + half * 8:blk * 16 + half * 8 + 8,
                         (dsl - 8) * 128:(dsl - 8) * 128 + 128],
                    in_=vt[half][:]), f"vsb{half}")

        preload_weights(True)

        J0 = ([("p", 0, t) for t in range(12)]
              + [("o", 0, 0), ("p", 0, 12), ("o", 0, 1), ("p", 0, 13),
                 ("o", 0, 2), ("p", 0, 14), ("o", 0, 3), ("p", 0, 15)])
        run_jobs(J0)

        J1p = [("p", 1, t) for t in range(16)]
        issue_job(J1p[0])
        issue_job(J1p[1])

        for blk in range(2):
            pn = panel[blk % 2]
            for dsl in range(16):
                par = dsl % 2
                ev_panel = panel_done[blk % 2]
                if blk == 0:
                    pl_ev = ln_job(J1p[dsl])
                    if dsl + 2 < 16:
                        issue_job(J1p[dsl + 2])
                em.wait(nc.tensor, "t", w_tok[par])
                em.wait(nc.tensor, "t", ev_panel)
                em.wait(nc.tensor, "t", kv_ps_free[0])
                for h in range(16):
                    for j in range(2):
                        t_ = nc.tensor.matmul(kv_ps[0][:][:, j * 512:(j + 1) * 512],
                                              wsl2[par][:][:, h, :],
                                              pn[:, h, j * 512:j * 512 + 512],
                                              start=(h == 0), stop=(h == 15))
                ev_pe_h0 = em.inc(t_, "pe")
                em.wait(nc.tensor, "t", wq_tok[par])
                em.wait(nc.tensor, "t", q_ps_free)
                em.wait(nc.tensor, "t", xnq_done[blk])
                if blk == 0 and dsl == 0:
                    em.wait(nc.tensor, "t", ev_p0)
                    em.wait(nc.tensor, "t", ev_p0_ac)
                for h in range(16):
                    t_ = nc.tensor.matmul(q_ps[:][:, 0:512],
                                          wslq[par][:][:, h, :],
                                          xnq_fm[:, h, :],
                                          start=(h == 0), stop=(h == 15))
                ev_pe_q = em.inc(t_, "pe")
                if blk == 0:
                    tr_job(J1p[dsl], pl_ev)
                emit_kv_evac(blk, dsl, 0, ev_pe_h0)
                flush_norms(2)
                em.wait(nc.tensor, "t", kv_ps_free[1])
                for h in range(16):
                    for j in range(2):
                        c0 = 1024 + j * 512
                        t_ = nc.tensor.matmul(kv_ps[1][:][:, j * 512:(j + 1) * 512],
                                              wsl2[par][:][:, h, :],
                                              pn[:, h, c0:c0 + 512],
                                              start=(h == 0), stop=(h == 15))
                ev_pe_h1 = em.inc(t_, "pe")
                if dsl + 2 < 16:
                    em.wait(nc.gpsimd, "g", ev_pe_h1)
                    w_tok[par] = em.dma(nc.gpsimd.dma_start(
                        out=wsl2[par][:],
                        in_=wqkvT[:, 2048 + (dsl + 2) * 128:2048 + (dsl + 3) * 128]
                            .rearrange("(hc p) d -> p hc d", p=128)), f"wl{par}")
                    wq_tok[par] = em.dma(nc.gpsimd.dma_start(
                        out=wslq[par][:],
                        in_=wqkvT[:, (dsl + 2) * 128:(dsl + 3) * 128]
                            .rearrange("(hc p) d -> p hc d", p=128)), f"ql{par}")
                em.wait(nc.vector, "v", ev_pe_q)
                em.wait(nc.vector, "v", em.tok("p0"))
                em.wait(nc.vector, "v", q_dma)
                em.wait(nc.vector, "v", q_sq_done)
                ev_qst = em.inc(nc.vector.tensor_scalar(
                    qstage[:], q_ps[:], bq_cols[:, dsl:dsl + 1], None,
                    op0=ALU.add), "ve")
                q_ps_free = ev_qst
                em.wait(nc.sync, "y", ev_qst)
                q_dma = em.dma(nc.sync.dma_start(
                    out=q_b[dsl * 128:(dsl + 1) * 128, blk * 512:(blk + 1) * 512],
                    in_=qstage[:]), "qsb")
                while any(e[0] == 'q' and e[3] == par for e in norm_pending):
                    flush_norms(len(norm_pending) - 1)
                em.wait(nc.scalar, "s", ev_qst)
                em.wait(nc.scalar, "s", qsq_mm[par])
                ev_qsq = em.inc(nc.scalar.activation(qsq[par][:], qstage[:],
                                                     AF.Square), "ac")
                q_sq_done = ev_qsq
                norm_pending.append(('q', dsl, blk, par, ev_qsq))
                emit_kv_evac(blk, dsl, 1, ev_pe_h1)
                flush_norms(2)
            flush_norms(0)
            if blk == 0:
                preload_weights(False)
                run_jobs([("o", 1, t) for t in range(4)])

    ev_p2_pe = em.tok("pe")
    ev_p2_ve = em.tok("ve")
    ev_kv_done = [em.tok("ksb0"), em.tok("ksb1"), em.tok("vsb0"), em.tok("vsb1")]
    ev_q_done = em.tok("qsb")

    at_stack = ExitStack()
    at_sb = at_stack.enter_context(nc.sbuf_tensor("at_sb", [128, 16, 1024], BF16))
    p2kv_end = [ev_p2_pe, ev_p2_ve, em.tok("ac"), em.tok("po"), em.tok("ksb0"),
                em.tok("ksb1"), em.tok("vsb0"), em.tok("vsb1"), em.tok("qsb"),
                em.tok("xla0"), em.tok("xla1"), em.tok("xlb0"), em.tok("xlb1"),
                em.tok("wl0"), em.tok("wl1"), em.tok("ql0"), em.tok("ql1"),
                em.tok("p0")]
    phase_barrier(p2kv_end)
    d = nc.gpsimd.dma_start(out=at_sb[:],
                            in_=at_in.rearrange("(hc p) a -> p hc a", p=128))
    ev_at = em.dma(d, "wmisc")

    b_stack = ExitStack()
    bsb = lambda n, s_, d_: b_stack.enter_context(nc.sbuf_tensor(n, s_, d_))
    k_sb = [bsb(f"k_sb{i}", [128, 2, S], BF16) for i in range(2)]
    v_sb = [bsb(f"v_sb{i}", [128, 32, 256], BF16) for i in range(2)]
    q_sb = [bsb(f"q_sb{i}", [128, 2, OWN], BF16) for i in range(2)]
    NEXP = 4
    expb = [[bsb(f"expb{i}_{c}", [128, S], BF16) for c in range(2)] for i in range(NEXP)]
    NPAR = 3
    pcomb2 = [bsb(f"pcomb2_{i}", [128, S], BF16) for i in range(NPAR)]
    pt_sb = [bsb(f"pt_sb{i}", [128, 8, 128], BF16) for i in range(2)]
    sums = [bsb(f"sums{i}", [128, 16], F32) for i in range(3)]
    o_st = [bsb(f"o_st{i}", [128, 256], BF16) for i in range(2)]
    osq = [bsb("osq0", [128, 256], F32), bsb("osq1", [128, 256], F32)]
    osqq = [bsb("osqq0", [128, 256], F32), bsb("osqq1", [128, 256], F32)]
    brow = bsb("brow", [1, 128], F32)
    ps_s = [b_stack.enter_context(nc.psum_tensor(f"ps_s{i}", [128, 1024], F32))
            for i in range(2)]
    ps_t = [b_stack.enter_context(nc.psum_tensor(f"ps_t{i}", [128, 8, 128], BF16))
            for i in range(2)]
    ps_o = [b_stack.enter_context(nc.psum_tensor(f"ps_o{i}", [128, 256], F32))
            for i in range(2)]

    def emit_B_variant(c):
        P_MAP = [4 * c + r for r in range(4)] + [4 * (7 - c) + r for r in range(4)]

        def load_group(g_, par, after_pe):
            em.wait(nc.sync, "y", after_pe)
            em.wait(nc.sync, "y", ev_p2_pe)
            em.wait(nc.sync, "y", ev_p2_ve)
            for tk in ev_kv_done:
                em.wait(nc.sync, "y", tk)
            d1 = nc.sync.dma_start(out=k_sb[par][:],
                                   in_=k_b[g_ * 256:(g_ + 1) * 256, :]
                                   .rearrange("(cc p) t -> p cc t", p=128))
            em.dma(d1, f"bg{par}")
            d2 = nc.sync.dma_start(out=v_sb[par][:],
                                   in_=v_b[:, g_ * 256:(g_ + 1) * 256]
                                   .rearrange("(tt p) e -> p tt e", p=128))
            return em.dma(d2, f"bg{par}")

        def load_q(dh, par, after_pe):
            em.wait(nc.sync, "y", after_pe)
            em.wait(nc.sync, "y", ev_q_done)
            d1 = nc.sync.dma_start(out=q_sb[par][:],
                                   in_=q_b[dh * 256:(dh + 1) * 256, :]
                                   .rearrange("(cc p) t -> p cc t", p=128))
            return em.dma(d1, f"bq{par}")

        grp_tok = [None] * NG
        q_tok = [None] * NDH
        grp_tok[0] = load_group(0, 0, None)
        q_tok[0] = load_q(0, 0, None)
        grp_tok[1] = load_group(1, 1, None)
        q_tok[1] = load_q(1, 1, None)
        em.wait(nc.vector, "v", ev_p2_ve)
        for dh in range(NDH):
            g_ = dh // 2
            for cc in range(2):
                gc = g_ * 2 + cc
                for l in range(8):
                    p = P_MAP[l]
                    col = (dh * 2 + cc) * 8 + l
                    nc.vector.tensor_reduce(
                        brow[0:1, col:col + 1],
                        knmax[0:1, gc * 32:gc * 32 + p + 1],
                        axis=AX.X, op=ALU.max)
        nc.vector.drain()
        ev_v = em.inc(nc.vector.tensor_mul(brow[:], brow[:], qn2row[:]), "ve")
        em.wait(nc.scalar, "s", ev_v)
        nc.scalar.activation(brow[:], brow[:], AF.Ln, bias=eps_col[0:1, :])
        nc.scalar.drain()
        ev_a = em.inc(nc.scalar.activation(brow[:], brow[:], AF.Exp, scale=0.5), "ac")
        em.wait(nc.vector, "v", ev_a)
        ev_v = em.inc(nc.vector.tensor_scalar_mul(brow[:], brow[:], -float(SCALE)), "ve")
        em.wait(nc.tensor, "t", ev_v)
        em.wait(nc.tensor, "t", ev_p2_pe)
        t = nc.tensor.matmul(ps_o[0][:, 0:128], ones_row[:], brow[:],
                             start=True, stop=True)
        ev_pe = em.inc(t, "pe")
        em.wait(nc.vector, "v", ev_pe)
        ev_bound = em.inc(nc.vector.tensor_copy(bound_bc[:], ps_o[0][:, 0:128]), "ve")

        units = [(dh, l) for dh in range(NDH) for l in range(8)]
        NU = len(units)
        exp_done = {}
        comb_ev = {}
        pv_done = {}
        exp_ps_free = {}
        exp_free = {}
        pcomb_free = {}
        pt_free = [None, None]
        o_free = [ev_bound, None]
        o_dma = [None, None]
        o_st_done = [None, None]
        last_scores = {}

        def emit_scores(u):
            dh, l = units[u]
            par = u % NEXP
            spar = u % 3
            g_ = dh // 2
            gpar = g_ % 2
            qpar = dh % 2
            n = P_MAP[l] + 1
            nch = cdiv(n, 8)
            em.wait(nc.tensor, "t", grp_tok[g_])
            em.wait(nc.tensor, "t", q_tok[dh])
            ev_e = None
            for cc in range(2):
                for ch in range(nch):
                    t0 = ch * 8
                    sz = min(8, n - t0)
                    spp = (cc * nch + ch) % 2
                    em.wait(nc.tensor, "t", exp_ps_free.get(spp))
                    for j0 in range(0, sz, 4):
                        w = min(4, sz - j0) * 128
                        t = nc.tensor.matmul(
                            ps_s[spp][:, j0 * 128:j0 * 128 + w],
                            q_sb[qpar][:, cc, l * 128:(l + 1) * 128],
                            k_sb[gpar][:, cc, (t0 + j0) * 128:(t0 + j0) * 128 + w],
                            start=True, stop=True)
                    ev_pe = em.inc(t, "pe")
                    ev_ready = ev_pe
                    if ch == nch - 1:
                        em.wait(nc.vector, "v", ev_pe)
                        mm = nc.vector.tensor_tensor(
                            ps_s[spp][:, (sz - 1) * 128:sz * 128],
                            ps_s[spp][:, (sz - 1) * 128:sz * 128],
                            cmask[:], op=ALU.add)
                        ev_ready = em.inc(mm, "ve")
                    em.wait(nc.scalar, "s", ev_ready)
                    em.wait(nc.scalar, "s", exp_free.get(par))
                    em.wait(nc.scalar, "s", ev_bound)
                    col = (dh * 2 + cc) * 8 + l
                    ea = nc.scalar.activation(
                        expb[par][cc][:, t0 * 128:t0 * 128 + sz * 128],
                        ps_s[spp][:, 0:sz * 128],
                        AF.Exp, bias=bound_bc[:, col:col + 1], scale=float(SCALE),
                        accum_out=sums[spar][:, cc * 4 + ch:cc * 4 + ch + 1])
                    ev_e = em.inc(ea, "ac")
                    exp_ps_free[spp] = ev_e
            last_scores[dh] = em.tok("pe")
            return ev_e

        def emit_combine(u):
            dh, l = units[u]
            par = u % NEXP
            spar = u % 3
            n = P_MAP[l] + 1
            L = n * 128
            nch = cdiv(n, 8)
            em.wait(nc.vector, "v", exp_done[u])
            nc.vector.tensor_reduce(sums[spar][:, 8:9], sums[spar][:, 0:nch],
                                    axis=AX.X, op=ALU.add)
            nc.vector.tensor_reduce(sums[spar][:, 9:10], sums[spar][:, 4:4 + nch],
                                    axis=AX.X, op=ALU.add)
            nc.vector.drain()
            nc.vector.reciprocal(sums[spar][:, 10:11], sums[spar][:, 8:9])
            nc.vector.reciprocal(sums[spar][:, 11:12], sums[spar][:, 9:10])
            nc.vector.drain()
            nc.vector.tensor_mul(sums[spar][:, 12:13], sums[spar][:, 8:9],
                                 sums[spar][:, 11:12])
            nc.vector.drain()
            em.wait(nc.vector, "v", pcomb_free.get(u % NPAR))
            ev_a2 = em.inc(nc.vector.tensor_scalar(pcomb2[u % NPAR][:, 0:L],
                                                   expb[par][1][:, 0:L],
                                                   sums[spar][:, 12:13], lamneg[:],
                                                   op0=ALU.mult, op1=ALU.mult), "ve")
            em.wait(nc.gpsimd, "g", ev_a2)
            ins = nc.gpsimd.tensor_tensor(pcomb2[u % NPAR][:, 0:L],
                                          pcomb2[u % NPAR][:, 0:L],
                                          expb[par][0][:, 0:L], op=ALU.add)
            ev = em.inc(ins, "po")
            exp_free[par] = ev
            return ev

        def emit_pv(u):
            dh, l = units[u]
            par = u % NPAR
            g_ = dh // 2
            gpar = g_ % 2
            n = P_MAP[l] + 1
            opar = u % 2
            em.wait(nc.tensor, "t", comb_ev[u])
            em.wait(nc.tensor, "t", o_free[opar])
            first = True
            last_pe = None
            for t0 in range(0, n, 8):
                sz = min(8, n - t0)
                tp = (t0 // 8) % 2
                em.wait(nc.tensor, "t", pt_free[tp])
                for jj in range(sz):
                    tt = nc.tensor.transpose(
                        ps_t[tp][:, jj, :],
                        pcomb2[par][:, (t0 + jj) * 128:(t0 + jj + 1) * 128],
                        ident_bf[:])
                ev_pe = em.inc(tt, "pe")
                h1 = (sz + 1) // 2
                em.wait(nc.vector, "v", ev_pe)
                ev_v1 = em.inc(nc.vector.tensor_copy(pt_sb[tp][:, 0:h1, :],
                                                     ps_t[tp][:, 0:h1, :]), "ve")
                if sz > h1:
                    ev_v2 = em.inc(nc.vector.tensor_copy(pt_sb[tp][:, h1:sz, :],
                                                         ps_t[tp][:, h1:sz, :]), "ve")
                else:
                    ev_v2 = ev_v1
                em.wait(nc.tensor, "t", ev_v1)
                for jj in range(h1):
                    mm = nc.tensor.matmul(ps_o[opar][:, :], pt_sb[tp][:, jj, :],
                                          v_sb[gpar][:, t0 + jj, :],
                                          start=first, stop=(t0 + jj == n - 1))
                    first = False
                em.wait(nc.tensor, "t", ev_v2)
                for jj in range(h1, sz):
                    mm = nc.tensor.matmul(ps_o[opar][:, :], pt_sb[tp][:, jj, :],
                                          v_sb[gpar][:, t0 + jj, :],
                                          start=first, stop=(t0 + jj == n - 1))
                    first = False
                last_pe = em.inc(mm, "pe")
                pt_free[tp] = last_pe
            pcomb_free[u % NPAR] = last_pe
            return last_pe

        def emit_subln(u):
            dh, l = units[u]
            opar = u % 2
            spar = u % 3
            em.wait(nc.vector, "v", pv_done[u])
            em.wait(nc.vector, "v", o_st_done[opar])
            ev_c = em.inc(nc.vector.tensor_scalar(osq[opar][:], ps_o[opar][:, :],
                                                  sums[spar][:, 10:11], None,
                                                  op0=ALU.mult), "ve")
            em.wait(nc.scalar, "s", ev_c)
            nc.scalar.activation(osqq[opar][:], osq[opar][:], AF.Square,
                                 accum_out=sums[spar][:, 13:14])
            nc.scalar.drain()
            nc.scalar.activation(sums[spar][:, 14:15], sums[spar][:, 13:14], AF.Ln,
                                 bias=eps_col[:], scale=1.0 / 256)
            nc.scalar.drain()
            ev_a = em.inc(nc.scalar.activation(sums[spar][:, 15:16],
                                               sums[spar][:, 14:15], AF.Exp,
                                               scale=-0.5), "ac")
            em.wait(nc.vector, "v", ev_a)
            nc.vector.tensor_scalar(osq[opar][:], osq[opar][:], sums[spar][:, 15:16],
                                    None, op0=ALU.mult)
            nc.vector.drain()
            em.wait(nc.vector, "v", o_dma[opar])
            ev_v = em.inc(nc.vector.tensor_mul(o_st[opar][:], osq[opar][:], subln_sc[:]), "ve")
            o_st_done[opar] = ev_v
            em.wait(nc.sync, "y", ev_v)
            o_dma[opar] = em.dma(nc.sync.dma_start(
                out=o_b[l * 128:(l + 1) * 128, dh * 256:(dh + 1) * 256],
                in_=o_st[opar][:]), f"osb{opar}")
            return ev_c

        for u in range(NU):
            dh, l = units[u]
            exp_done[u] = emit_scores(u)
            if u >= 1:
                comb_ev[u - 1] = emit_combine(u - 1)
            if u >= 2:
                pv_done[u - 2] = emit_pv(u - 2)
            if u >= 3:
                o_free[(u - 3) % 2] = emit_subln(u - 3)
            if l == 1:
                g_ = dh // 2
                if dh % 2 == 0 and g_ + 1 < NG and grp_tok[g_ + 1] is None:
                    grp_tok[g_ + 1] = load_group(g_ + 1, (g_ + 1) % 2,
                                                 pv_done.get((2 * g_) * 8 - 1))
                if dh + 1 < NDH and q_tok[dh + 1] is None:
                    q_tok[dh + 1] = load_q(dh + 1, (dh + 1) % 2,
                                           last_scores.get(dh - 1))
        u = NU - 1
        comb_ev[u] = emit_combine(u)
        pv_done[u - 1] = emit_pv(u - 1)
        o_free[(u - 2) % 2] = emit_subln(u - 2)
        pv_done[u] = emit_pv(u)
        o_free[(u - 1) % 2] = emit_subln(u - 1)
        o_free[u % 2] = emit_subln(u)

    for _ch in ("bg0", "bg1", "bq0", "bq1", "osb0", "osb1"):
        em.ensure(_ch)
    pre_state = em.snap_state()
    ends = []
    for v in nc.Switch(engines=EV, index=[idxv] * len(EV), n=4):
        em.restore_state(pre_state)
        emit_B_variant(v)
        ends.append(dict(em.cnt))
    maxc = {q: max(e[q] for e in ends) for q in ends[0]}
    for qn in ("osb0", "osb1", "bg0", "bg1", "bq0", "bq1", "po"):
        assert all(e[qn] == ends[0][qn] for e in ends), qn
    eng_of = {"pe": nc.tensor, "ve": nc.vector, "ac": nc.scalar}
    for v in nc.Switch(engines=EV, index=[idxv] * len(EV), n=4):
        for qn in ("pe", "ve", "ac"):
            delta = maxc[qn] - ends[v][qn]
            if delta > 0:
                eng_of[qn].sem_inc(em.sems[qn], delta)
    em.restore_state(pre_state)
    em.cnt = dict(maxc)
    em.seen = {}
    b_stack.close()
    ev_B_pe = em.tok("pe")
    ev_B_ve = em.tok("ve")
    ev_o_done = [em.tok("osb0"), em.tok("osb1")]

    with ExitStack() as c12:
        cb = lambda n, s_, d_: c12.enter_context(nc.sbuf_tensor(n, s_, d_))
        xc = [cb("xc0", [128, HID], F32), cb("xc1", [128, HID], F32)]
        stat3 = cb("stat3", [128, 16], F32)
        xn_tm3 = cb("xn_tm3", [128, HID], BF16)
        xn_fm3 = cb("xn_fm3", [128, 16, 128], BF16)
        vsq = cb("vsq", [128, 1024], F32)
        arr2 = cb("arr2", [128, 64], F32)
        top8 = cb("top8", [128, 8], F32)
        zap = cb("zap", [128, 64], F32)
        routes = cb("routes", [128, 64], F32)
        vecs_bf = cb("vecs_bf", [128, 1024], BF16)
        vms = cb("vms", [128, 1024], F32)
        vmix = cb("vmix", [128, 16], F32)
        w_bf = cb("w_bf", [128, 1024], BF16)
        wfm_st = cb("wfm_st", [128, 8, 128], BF16)
        outw_sb = cb("outw_sb", [128, 16, HID], BF16)
        outbb = cb("outbb", [128, HID], F32)
        rowb = cb("rowb", [1, HID], F32)
        o_tm = [cb("o_tm0", [128, HID], BF16), cb("o_tm1", [128, HID], BF16)]
        o_fm = cb("o_fm", [128, 16, 128], BF16)
        s_scr = [cb("s_scr0", [128, HID], BF16), cb("s_scr1", [128, HID], BF16)]
        ps_vx = c12.enter_context(nc.psum_tensor("ps_vx", [128, 1024], F32))
        ps_s4 = c12.enter_context(nc.psum_tensor("ps_s4", [128, HID], F32))
        tr_ps3 = [c12.enter_context(nc.psum_tensor(f"tr_ps3{i}", [128, 4, 128], BF16)) for i in range(2)]

        phase_barrier([ev_B_pe, ev_B_ve, em.tok("ac"), em.tok("bg0"), em.tok("bg1"),
                       em.tok("bq0"), em.tok("bq1")] + ev_o_done)
        em.wait(nc.gpsimd, "g", em.tok("wmisc"))
        outw_src = outwT_in.rearrange("(hc p) d -> p hc d", p=128)
        tok_w_q = [em.dma(nc.gpsimd.dma_start(
            out=outw_sb[:, qq * 4:(qq + 1) * 4, :],
            in_=outw_src[:, qq * 4:(qq + 1) * 4, :]), ch)
            for qq, ch in enumerate(("wmisc", "wq1", "wq2", "wq3"))]
        em.wait(nc.sync, "y", ev_B_pe)
        em.wait(nc.sync, "y", ev_B_ve)
        tok_b = em.dma(nc.sync.dma_start(out=rowb[:], in_=outb_in[:]), "rw2")
        em.wait(nc.tensor, "t", tok_b)
        ev_ob = emit_row_bcast(rowb, ps_vx[:, 0:512], outbb, HID)

        trp = TrPool(tr_ps3)
        ln_last = [None, None]
        x_tok = {}
        o_tok = {}
        vx_free = [ev_ob]
        vecs_mm = None
        s4_mm = None
        ps4_free = None
        w_read = None
        wf_dma = None
        epi_ve = None
        tr_pe3 = None
        s_dma = [None, None]
        o_consumed = [None, None]

        def issue_x(t):
            par = t % 2
            em.wait(nc.sync, "y", ln_last[par])
            x_tok[t] = em.dma(nc.sync.dma_start(
                out=xc[par][:], in_=x_own[t * 128:(t + 1) * 128, :]), f"xl{par}")

        def issue_o(t):
            par = t % 2
            em.wait(nc.sync, "y", o_consumed[par])
            for tk in ev_o_done:
                em.wait(nc.sync, "y", tk)
            o_tok[t] = em.dma(nc.sync.dma_start(
                out=o_tm[par][:], in_=o_b[t * 128:(t + 1) * 128, :]), f"ol{par}")

        def emit_c1_epilogue(t):
            nonlocal w_read, wf_dma, epi_ve, vx_free
            em.wait(nc.scalar, "s", vecs_mm)
            em.wait(nc.scalar, "s", epi_ve)
            ev_sqv = em.inc(nc.scalar.activation(vsq[:], ps_vx[:], AF.Square), "ac")
            em.wait(nc.vector, "v", ev_sqv)
            nc.vector.tensor_reduce(arr2[:], vsq[:].rearrange("p (a r) -> p a r", r=16),
                                    axis=AX.X, op=ALU.add)
            nc.vector.drain()
            nc.vector.max(out=top8[:], in_=arr2[:])
            nc.vector.drain()
            nc.vector.memset(top8[:, 4:8], -1.0)
            nc.vector.drain()
            nc.vector.match_replace(out=zap[:], in_to_replace=top8[:],
                                    in_values=arr2[:], imm_value=-1.0)
            nc.vector.drain()
            nc.vector.tensor_scalar(routes[:], zap[:], -1.0, 0.25,
                                    op0=ALU.is_le, op1=ALU.mult)
            ev_vec = em.inc(nc.vector.tensor_copy(vecs_bf[:], ps_vx[:]), "ve")
            vx_free = [ev_sqv, ev_vec]
            nc.vector.drain()
            nc.vector.tensor_tensor(
                vms[:].rearrange("p (r a) -> p r a", a=64),
                vecs_bf[:].rearrange("p (a r) -> p r a", r=16),
                routes[:].rearrange("p a -> p () a").to_broadcast([128, 16, 64]),
                op=ALU.mult)
            nc.vector.drain()
            nc.vector.tensor_reduce(vmix[:], vms[:].rearrange("p (r a) -> p r a", a=64),
                                    axis=AX.X, op=ALU.add)
            nc.vector.drain()
            em.wait(nc.vector, "v", w_read)
            ev_w = em.inc(nc.vector.tensor_tensor(
                w_bf[:].rearrange("p (a r) -> p a r", r=16),
                routes[:].to_broadcast([128, 64, 16]),
                vmix[:].rearrange("p r -> p () r").to_broadcast([128, 64, 16]),
                op=ALU.mult), "ve")
            em.wait(nc.tensor, "t", ev_w)
            ev_t = trp.run([w_bf[:, ar * 128:(ar + 1) * 128] for ar in range(8)],
                           [wfm_st[:, g0:g0 + 4, :] for g0 in range(0, 8, 4)],
                           extra_v_wait=wf_dma)
            w_read = em.tok("pe")
            epi_ve = em.tok("ve")
            em.wait(nc.sync, "y", ev_t)
            wf_dma = em.dma(nc.sync.dma_start(
                out=wfm_b.rearrange("(c p) t -> p c t", p=128)[:, :, t * 128:(t + 1) * 128],
                in_=wfm_st[:]), "wf")

        def emit_c2_epilogue(t):
            nonlocal ps4_free
            par = t % 2
            em.wait(nc.vector, "v", s4_mm)
            em.wait(nc.vector, "v", ev_ob)
            em.wait(nc.vector, "v", s_dma[par])
            ev_s = em.inc(nc.vector.tensor_add(s_scr[par][:], ps_s4[:, :], outbb[:]), "ve")
            ps4_free = ev_s
            em.wait(nc.sync, "y", ev_s)
            s_dma[par] = em.dma(nc.sync.dma_start(
                out=s_b[t * 128:(t + 1) * 128, :], in_=s_scr[par][:]), f"ssb{par}")

        issue_x(0)
        issue_o(0)
        issue_o(1)
        for t in range(8):
            par = t % 2
            em.wait(nc.scalar, "s", x_tok[t])
            em.wait(nc.vector, "v", x_tok[t])
            ev_v = emit_ln(xc[par][:], tr_pe3, stat3[:, par * 8:par * 8 + 8], xn_tm3)
            ln_last[par] = ev_v
            if t + 1 < 8:
                issue_x(t + 1)
            if t >= 1:
                emit_c1_epilogue(t - 1)
                emit_c2_epilogue(t - 1)
            em.wait(nc.tensor, "t", o_tok[t])
            ev2 = trp.run([o_tm[par][:][:, hc * 128:(hc + 1) * 128] for hc in range(16)],
                          [o_fm[:, g0:g0 + 4, :] for g0 in range(0, 16, 4)],
                          extra_v_wait=s4_mm)
            em.wait(nc.tensor, "t", ev2)
            em.wait(nc.tensor, "t", ps4_free)
            for hc in range(16):
                em.wait(nc.tensor, "t", tok_w_q[hc // 4])
                for j in range(4):
                    mm = nc.tensor.matmul(ps_s4[:, j * 512:(j + 1) * 512],
                                          o_fm[:, hc, :],
                                          outw_sb[:, hc, j * 512:(j + 1) * 512],
                                          start=(hc == 0), stop=(hc == 15))
            ev_pe2 = em.inc(mm, "pe")
            s4_mm = ev_pe2
            o_consumed[par] = ev_pe2
            if t + 2 < 8:
                issue_o(t + 2)
            em.wait(nc.tensor, "t", ev_v)
            ev = trp.run([xn_tm3[:, hc * 128:(hc + 1) * 128] for hc in range(16)],
                         [xn_fm3[:, g0:g0 + 4, :] for g0 in range(0, 16, 4)],
                         extra_v_wait=vecs_mm)
            tr_pe3 = em.tok("pe")
            em.wait(nc.tensor, "t", ev)
            em.wait(nc.tensor, "t", ev_at)
            for tk in vx_free:
                em.wait(nc.tensor, "t", tk)
            for h in range(16):
                for j in range(2):
                    t2 = nc.tensor.matmul(ps_vx[:][:, j * 512:(j + 1) * 512],
                                          xn_fm3[:, h, :],
                                          at_sb[:, h, j * 512:(j + 1) * 512],
                                          start=(h == 0), stop=(h == 15))
            ev_pe = em.inc(t2, "pe")
            vecs_mm = ev_pe
        emit_c1_epilogue(7)
        emit_c2_epilogue(7)
    at_stack.close()
    ev_c2_pe = em.tok("pe")
    ev_c2_ve = em.tok("ve")
    ev_c2_ac = em.tok("ac")

    with ExitStack() as c3:
        cb = lambda n, s_, d_: c3.enter_context(nc.sbuf_tensor(n, s_, d_))
        wt_sb = cb("wt_sb", [128, 16, HID], BF16)
        b2_sb = cb("b2_sb", [128, 8, HID], BF16)
        ffnbb = cb("ffnbb", [128, HID], F32)
        xc2 = [cb("xc20", [128, HID], F32), cb("xc21", [128, HID], F32)]
        lnscr4 = cb("lnscr4", [128, HID], F32)
        stat4 = cb("stat4", [128, 16], F32)
        xn_tm4 = cb("xn_tm4", [128, HID], BF16)
        xn_fm4 = cb("xn_fm4", [128, 16, 128], BF16)
        wfm_ld = [cb("wfm_ld0", [128, 8, 128], BF16), cb("wfm_ld1", [128, 8, 128], BF16)]
        s_tile = [cb("s_tile0", [128, HID], BF16), cb("s_tile1", [128, HID], BF16)]
        g_scr = cb("g_scr", [128, HID], F32)
        sg_scr = cb("sg_scr", [128, HID], F32)
        y_st = [cb("y_st0", [128, HID], F32), cb("y_st1", [128, HID], F32)]
        bd = [c3.enter_context(nc.psum_tensor(f"ps_bd{i}", [128, 1024], F32))
              for i in range(2)]
        tr_ps5 = [c3.enter_context(nc.psum_tensor(f"tr_ps5{i}", [128, 4, 128], BF16)) for i in range(2)]
        p0w3 = c3.enter_context(nc.psum_tensor("p0w3", [128, 512], F32))

        phase_barrier([ev_c2_pe, ev_c2_ve, ev_c2_ac, em.tok("ol0"), em.tok("ol1"),
                       em.tok("ssb0"), em.tok("ssb1"), em.tok("rw2"),
                       em.tok("wmisc"), em.tok("wq1"), em.tok("wq2"),
                       em.tok("wq3"), em.tok("xl0"), em.tok("xl1"),
                       em.tok("wf"), em.tok("po")])
        em.wait(nc.gpsimd, "g", em.tok("wmisc"))
        wt_src = wt_in.rearrange("(hc p) d -> p hc d", p=128)
        tok_wt_q = [em.dma(nc.gpsimd.dma_start(
            out=wt_sb[:, qq * 4:(qq + 1) * 4, :],
            in_=wt_src[:, qq * 4:(qq + 1) * 4, :]), ch)
            for qq, ch in enumerate(("wmisc", "wq1", "wq2", "wq3"))]
        b2_src = b2_in.rearrange("(ac p) d -> p ac d", p=128)
        tok_b2_q = [em.dma(nc.gpsimd.dma_start(
            out=b2_sb[:, qq * 4:(qq + 1) * 4, :],
            in_=b2_src[:, qq * 4:(qq + 1) * 4, :]), ch)
            for qq, ch in enumerate(("wb1", "wb2"))]
        em.wait(nc.sync, "y", ev_c2_pe)
        em.wait(nc.sync, "y", ev_c2_ve)
        em.wait(nc.sync, "y", ev_c2_ac)
        tok_fb = em.dma(nc.sync.dma_start(out=lnscr4[0:1, :], in_=fb_in[:]), "rw3")
        em.wait(nc.tensor, "t", tok_fb)
        ev_fbb = emit_row_bcast(lnscr4, p0w3, ffnbb, HID)
        fbb_pe = em.tok("pe")

        trp = TrPool(tr_ps5)
        ln_last = [None, None]
        s_tok = {}
        x_tok = {}
        wf_tok = {}
        y_dma = [None, None]
        eps_free = [None, None]
        bd_mm_s = {}
        tr_pe5 = None
        tr_t = {}

        def issue_tile(t):
            par = t % 2
            em.wait(nc.sync, "y", ln_last[par])
            em.wait(nc.sync, "y", bd_mm_s.get(2 * t - 3))
            if t == 0:
                em.wait(nc.sync, "y", em.tok("wf"))
            wf_tok[t] = em.dma(nc.sync.dma_start(
                out=wfm_ld[par][:],
                in_=wfm_b.rearrange("(c p) t -> p c t", p=128)[:, :, t * 128:(t + 1) * 128]),
                f"wfl{par}")
            x_tok[t] = em.dma(nc.sync.dma_start(out=xc2[par][:],
                                                in_=x_own[t * 128:(t + 1) * 128, :]),
                              f"xl{par}")
            for tk in (em.tok("ssb0"), em.tok("ssb1")) if t < 2 else ():
                em.wait(nc.sync, "y", tk)
            s_tok[t] = em.dma(nc.sync.dma_start(
                out=s_tile[par][:], in_=s_b[t * 128:(t + 1) * 128, :]), f"sl{par}")

        def emit_c3_epilogue(s):
            t, hj = divmod(s, 2)
            par = t % 2
            b = s % 2
            cols = slice(hj * 1024, (hj + 1) * 1024)
            nc.vector.drain()
            em.wait(nc.vector, "v", bd_mm_s[s])
            em.wait(nc.vector, "v", ev_fbb)
            ev_bd = em.inc(nc.vector.tensor_add(sg_scr[:, cols], bd[b][:, :],
                                                ffnbb[:, cols]), "ve")
            eps_free[b] = ev_bd
            em.wait(nc.scalar, "s", ev_bd)
            ev_g = em.inc(nc.scalar.activation(g_scr[:, cols], sg_scr[:, cols], AF.Exp,
                                               scale=-1.0), "ac")
            em.wait(nc.vector, "v", ev_g)
            nc.vector.tensor_scalar_add(g_scr[:, cols], g_scr[:, cols], 1.0)
            nc.vector.drain()
            nc.vector.reciprocal(g_scr[:, cols], g_scr[:, cols])
            nc.vector.drain()
            nc.vector.tensor_mul(g_scr[:, cols], g_scr[:, cols], sg_scr[:, cols])
            nc.vector.drain()
            em.wait(nc.vector, "v", s_tok[t])
            nc.vector.tensor_mul(sg_scr[:, cols], s_tile[par][:, cols], g_scr[:, cols])
            nc.vector.drain()
            em.wait(nc.vector, "v", y_dma[par])
            ev_y = em.inc(nc.vector.tensor_add(y_st[par][:, cols], sg_scr[:, cols],
                                               xc2[par][:, cols]), "ve")
            if hj == 1:
                ln_last[par] = ev_y
            em.wait(nc.sync, "y", ev_y)
            y_dma[par] = em.dma(nc.sync.dma_start(
                out=y_out[t * 128:(t + 1) * 128, cols], in_=y_st[par][:, cols]),
                f"yb{par}")

        issue_tile(0)
        issue_tile(1)
        for s in range(16):
            t, hj = divmod(s, 2)
            par = t % 2
            b = s % 2
            if hj == 0:
                em.wait(nc.scalar, "s", x_tok[t])
                em.wait(nc.scalar, "s", fbb_pe)
                em.wait(nc.vector, "v", x_tok[t])
                ev_v = emit_ln(xc2[par][:], tr_pe5, stat4[:, par * 8:par * 8 + 8],
                               xn_tm4)
                em.wait(nc.tensor, "t", ev_v)
                ev = trp.run([xn_tm4[:, hc * 128:(hc + 1) * 128] for hc in range(16)],
                             [xn_fm4[:, g0:g0 + 4, :] for g0 in range(0, 16, 4)],
                             extra_v_wait=bd_mm_s.get(2 * t - 1))
                tr_pe5 = em.tok("pe")
                tr_t[t] = ev
            em.wait(nc.tensor, "t", tr_t[t])
            em.wait(nc.tensor, "t", eps_free[b])
            for hc in range(16):
                em.wait(nc.tensor, "t", tok_wt_q[hc // 4])
                for j in range(2):
                    mm = nc.tensor.matmul(bd[b][:, j * 512:(j + 1) * 512],
                                          xn_fm4[:, hc, :],
                                          wt_sb[:, hc, hj * 1024 + j * 512:hj * 1024 + (j + 1) * 512],
                                          start=(hc == 0), stop=False)
            em.wait(nc.tensor, "t", wf_tok[t])
            for ar in range(8):
                em.wait(nc.tensor, "t", tok_b2_q[ar // 4])
                for j in range(2):
                    mm = nc.tensor.matmul(bd[b][:, j * 512:(j + 1) * 512],
                                          wfm_ld[par][:][:, ar, :],
                                          b2_sb[:, ar, hj * 1024 + j * 512:hj * 1024 + (j + 1) * 512],
                                          start=False, stop=(ar == 7))
            bd_mm_s[s] = em.inc(mm, "pe")
            if s >= 1:
                emit_c3_epilogue(s - 1)
                if s % 2 == 0 and s // 2 + 1 <= 7:
                    issue_tile(s // 2 + 1)
        emit_c3_epilogue(15)
        em.wait(nc.sync, "y", em.tok("yb0"))
        em.wait(nc.sync, "y", em.tok("yb1"))

    stack.close()
    return nc


_NC_CACHE = [None]


def _prep(inputs):
    hid = np.asarray(inputs["L0_attn_hidden_states"], dtype=np.float32)
    wqkvT = np.ascontiguousarray(np.asarray(inputs["L0_attn_Wqkv_w"], np.float32).T)
    bq = np.ascontiguousarray(np.asarray(inputs["L0_attn_Wqkv_b"], np.float32).reshape(1, 4096))
    lrows = np.ascontiguousarray(np.stack([
        np.asarray(inputs["L0_attn_lambda_q1"], np.float32),
        np.asarray(inputs["L0_attn_lambda_k1"], np.float32),
        np.asarray(inputs["L0_attn_lambda_q2"], np.float32),
        np.asarray(inputs["L0_attn_lambda_k2"], np.float32)]))
    subln = np.ascontiguousarray(np.asarray(inputs["L0_attn_subln_w"], np.float32).reshape(1, 256))
    outwT = np.ascontiguousarray(np.asarray(inputs["L0_attn_out_w"], np.float32).T)
    outb = np.ascontiguousarray(np.asarray(inputs["L0_attn_out_b"], np.float32).reshape(1, 2048))
    A = np.asarray(inputs["L0_ffn_A"], np.float32)
    Bm = np.asarray(inputs["L0_ffn_B"], np.float32)
    Wf = np.asarray(inputs["L0_ffn_W"], np.float32)
    fb = np.ascontiguousarray(np.asarray(inputs["L0_ffn_bias"], np.float32).reshape(1, 2048))
    a_t = np.ascontiguousarray(A.reshape(1024, 2048).T)
    b2 = np.ascontiguousarray(Bm.transpose(0, 2, 1).reshape(1024, 2048))
    w_t = np.ascontiguousarray(Wf.T)

    in_maps = []
    rows_per_core = []
    for i in range(8):
        b = i // 4
        c = i % 4
        rows = np.r_[512 * c:512 * (c + 1), 512 * (7 - c):512 * (8 - c)]
        rows_per_core.append((b, rows))
        in_maps.append({
            "x_full": np.ascontiguousarray(hid[b]),
            "x_own": np.ascontiguousarray(hid[b][rows]),
            "wqkvT": wqkvT, "bqkv": bq, "lrows": lrows, "subln": subln,
            "outwT": outwT, "outb": outb, "a_t": a_t, "b2": b2,
            "w_t": w_t, "fbias": fb,
        })
    return in_maps, rows_per_core


def kernel(**inputs):
    in_maps, rows_per_core = _prep(inputs)
    if _NC_CACHE[0] is None:
        _NC_CACHE[0] = build_program()
    nc = _NC_CACHE[0]
    res = run_bass_kernel_spmd(nc, in_maps, core_ids=list(range(8)))
    out = np.zeros((2, 4096, 2048), np.float32)
    for i in range(8):
        b, rows = rows_per_core[i]
        out[b][rows] = np.asarray(res.results[i]["y"])
    return out


if __name__ == "__main__":
    import time
    t0 = time.time()
    nc = build_program()
    print("built instructions:", nc.n_instructions(), "in", round(time.time() - t0, 1), "s")
